# version 01
# baseline (speedup 1.0000x reference)
import numpy as np
import ml_dtypes
import contextlib
import concourse.bass as bass
import concourse.mybir as mybir
from concourse.bass_utils import run_bass_kernel_spmd

F32 = mybir.dt.float32
BF16 = mybir.dt.bfloat16
I32 = mybir.dt.int32
AF = mybir.ActivationFunctionType
ALU = mybir.AluOpType
AX = mybir.AxisListType
BF = ml_dtypes.bfloat16

NCORES = 8
B, S, D = 2, 8192, 1024
N = B * S
TPC = N // NCORES
NT = TPC // 128
NST = TPC // 512
HEADS = 16
HD = 64
UNITS_PER_CORE = (B * HEADS) // NCORES

ENG2BLK = dict(pe='tensor', act='scalar', dve='vector', pool='gpsimd', sp='sync')


class Op:
    __slots__ = ('eng', 'method', 'kw', 'deps', 'dma', 'inc', 'pos', 'outdma')

    def __init__(self, eng, method, kw, deps, dma):
        self.eng, self.method, self.kw, self.deps, self.dma = eng, method, kw, deps, dma
        self.inc = False
        self.pos = 0
        self.outdma = False


def _key(x):
    t = getattr(x, 'tensor', None)
    if t is not None:
        return t.name
    return x.name


class KB:

    def __init__(self, tag=""):
        self.nc = bass.Bass("TRN2", target_bir_lowering=False)
        self.es = contextlib.ExitStack()
        self.ops = []
        self.lastw = {}
        self.readers = {}
        self.consts = {}
        self.out_names = []
        self.sb_bytes = 0
        self.tag = tag

    def din(self, name, shape, dt=F32):
        return self.nc.dram_tensor(name, list(shape), dt, kind="ExternalInput").ap()

    def dout(self, name, shape, dt=F32):
        self.out_names.append(name)
        return self.nc.dram_tensor(name, list(shape), dt, kind="ExternalOutput").ap()

    def const(self, name, arr):
        arr = np.ascontiguousarray(arr)
        dt = {np.dtype(np.float32): F32, np.dtype(BF): BF16,
              np.dtype(np.int32): I32}[arr.dtype]
        self.consts[name] = arr
        return self.nc.dram_tensor(name, list(arr.shape), dt, kind="ExternalInput").ap()

    def constsb(self, name, arr):
        arr = np.ascontiguousarray(arr)
        dt = {np.dtype(np.float32): F32, np.dtype(BF): BF16,
              np.dtype(np.int32): I32}[arr.dtype]
        dr = self.const(name + '_c', arr)
        sbh = self.sb(name, list(arr.shape), dt)
        self.dma('sp', sbh[:], dr)
        return sbh

    def sb(self, name, shape, dt=F32):
        esz = 2 if dt == BF16 else 4
        self.sb_bytes += shape[1] * esz * 128
        return self.es.enter_context(self.nc.sbuf_tensor(name, list(shape), dt))

    def ps(self, name, shape, dt=F32):
        return self.es.enter_context(self.nc.psum_tensor(name, list(shape), dt))

    def _rec(self, eng, method, kw, reads, writes, dma=False):
        idx = len(self.ops)
        deps = set()
        for r in reads:
            if r is None:
                continue
            k = _key(r)
            lw = self.lastw.get(k)
            if lw is not None and not (self.ops[lw].eng == 'pe' and eng == 'pe' and not self.ops[lw].dma and not dma):
                deps.add(lw)
        for w in writes:
            k = _key(w)
            lw = self.lastw.get(k)
            if lw is not None and not (self.ops[lw].eng == 'pe' and eng == 'pe' and not self.ops[lw].dma and not dma):
                deps.add(lw)
            for rd in self.readers.get(k, ()):
                if not (self.ops[rd].eng == 'pe' and eng == 'pe' and not self.ops[rd].dma and not dma):
                    deps.add(rd)
        for w in writes:
            k = _key(w)
            self.lastw[k] = idx
            self.readers[k] = []
        for r in reads:
            if r is None:
                continue
            self.readers.setdefault(_key(r), []).append(idx)
        op = Op(eng, method, kw, deps, dma)
        self.ops.append(op)
        return op

    def mm(self, out, lhsT, rhs, start=True, stop=True):
        self._rec('pe', 'matmul', dict(out=out, lhsT=lhsT, rhs=rhs, start=start, stop=stop),
                  reads=[lhsT, rhs], writes=[out])

    def transp(self, out, in_, ident):
        p = in_.shape[0]
        idap = ident[0:p, 0:p] if ident.shape[0] != p else ident[:, 0:p]
        self._rec('pe', 'transpose', dict(out=out, in_=in_, identity=idap),
                  reads=[in_, ident], writes=[out])

    def act(self, out, in_, func, bias=None, scale=None, accum_out=None):
        kw = dict(out=out, in_=in_, func=func)
        reads = [in_]
        writes = [out]
        if bias is not None:
            kw['bias'] = bias
            if not isinstance(bias, (int, float)):
                reads.append(bias)
        if scale is not None:
            kw['scale'] = scale
            if not isinstance(scale, (int, float)):
                reads.append(scale)
        if accum_out is not None:
            kw['accum_out'] = accum_out
            writes.append(accum_out)
        self._rec('act', 'activation', kw, reads=reads, writes=writes)

    def tt(self, out, a, b, op):
        self._rec('dve', 'tensor_tensor', dict(out=out, in0=a, in1=b, op=op),
                  reads=[a, b], writes=[out])

    def ts(self, out, a, s1, op0, s2=None, op1=None):
        kw = dict(out=out, in0=a, scalar1=s1, scalar2=s2, op0=op0)
        if op1 is not None:
            kw['op1'] = op1
        reads = [a]
        if not isinstance(s1, (int, float)):
            reads.append(s1)
        if s2 is not None and not isinstance(s2, (int, float)):
            reads.append(s2)
        self._rec('dve', 'tensor_scalar', kw, reads=reads, writes=[out])

    def tss(self, out, a, s, op):
        reads = [a]
        if not isinstance(s, (int, float)):
            reads.append(s)
        self._rec('dve', 'tensor_single_scalar', dict(out=out, in_=a, scalar=s, op=op),
                  reads=reads, writes=[out])

    def stt(self, out, in0, s, in1, op0, op1):
        reads = [in0, in1]
        if not isinstance(s, (int, float)):
            reads.append(s)
        self._rec('dve', 'scalar_tensor_tensor', dict(out=out, in0=in0, scalar=s, in1=in1, op0=op0, op1=op1),
                  reads=reads, writes=[out])

    def red(self, out, in_, op, absval=False, negate=False):
        kw = dict(out=out, in_=in_, axis=AX.X, op=op)
        if absval:
            kw['apply_absolute_value'] = True
        if negate:
            kw['negate'] = True
        self._rec('dve', 'tensor_reduce', kw, reads=[in_], writes=[out])

    def recip(self, out, in_):
        self._rec('dve', 'reciprocal', dict(out=out, in_=in_), reads=[in_], writes=[out])

    def dcopy(self, out, in_):
        self._rec('dve', 'tensor_copy', dict(out=out, in_=in_), reads=[in_], writes=[out])

    def memset(self, ap, c):
        self._rec('dve', 'memset', dict(ap=ap, constant=c), reads=[], writes=[ap])

    def dma(self, q, out, in_, outdma=False):
        op = self._rec(q, 'dma_start', dict(out=out, in_=in_), reads=[in_], writes=[out], dma=True)
        op.outdma = outdma
        return op

    def idma_gather(self, out, table, idx, bound, outdma=False):
        op = self._rec('pool', 'indirect_dma_start',
                       dict(out=out, out_offset=None, in_=table,
                            in_offset=bass.IndirectOffsetOnAxis(ap=idx, axis=0),
                            bounds_check=bound, oob_is_err=False),
                       reads=[table, idx], writes=[out], dma=True)
        op.outdma = outdma
        return op

    def build(self):
        ops = self.ops
        for op in ops:
            if op.dma:
                op.inc = True
        for op in ops:
            for d in op.deps:
                ops[d].inc = True
        cnt = {}
        for op in ops:
            k = (op.eng, op.dma)
            if op.inc:
                cnt[k] = cnt.get(k, 0) + (16 if op.dma else 1)
            op.pos = cnt.get(k, 0)
        sem_keys = sorted(set((op.eng, op.dma) for op in ops if op.inc))
        finwaits = {}
        for op in ops:
            if op.outdma:
                k = (op.eng, op.dma)
                finwaits[k] = max(finwaits.get(k, 0), op.pos)
        assert finwaits, "no output DMAs marked"

        sems = {}
        for k in sem_keys:
            sems[k] = self.es.enter_context(self.nc.semaphore(f"s_{k[0]}_{'d' if k[1] else 'c'}"))

        blk = self.es.enter_context(self.nc.Block())
        per_eng = {}
        for i, op in enumerate(ops):
            per_eng.setdefault(op.eng, []).append(op)

        def make_runner(eng, eops, is_fin):
            def runner(e):
                waited = {}
                for op in eops:
                    for d in sorted(op.deps):
                        dep = ops[d]
                        k = (dep.eng, dep.dma)
                        v = dep.pos
                        assert dep.inc
                        if waited.get(k, 0) < v:
                            e.wait_ge(sems[k], v)
                            waited[k] = v
                    inst = getattr(e, op.method)(**op.kw)
                    if op.inc:
                        inst.then_inc(sems[(op.eng, op.dma)], 16 if op.dma else 1)
                if is_fin:
                    for k, v in sorted(finwaits.items()):
                        if waited.get(k, 0) < v or True:
                            e.wait_ge(sems[k], v)
            return runner

        fin_eng = 'pool' if 'pool' in per_eng else sorted(per_eng)[0]
        for eng, eops in per_eng.items():
            getattr(blk, ENG2BLK[eng])(make_runner(eng, eops, eng == fin_eng))
        if fin_eng not in per_eng:
            getattr(blk, ENG2BLK[fin_eng])(make_runner(fin_eng, [], True))
        self.es.close()
        return self.nc

    def run(self, per_core_inputs, trace=False):
        nc = self.build()
        in_maps = []
        for ci in range(len(per_core_inputs)):
            m = dict(per_core_inputs[ci])
            for cn, arr in self.consts.items():
                m[cn] = arr
            in_maps.append(m)
        res = run_bass_kernel_spmd(nc, in_maps, core_ids=list(range(len(per_core_inputs))), trace=trace)
        return res.results



def host_consts():
    c = {}
    c['ident_f'] = np.eye(128, dtype=np.float32)
    c['ident_b'] = np.eye(128, dtype=np.float32).astype(BF)
    c['ones_row_b'] = np.ones((1, 128), dtype=np.float32).astype(BF)
    c['ones_row_f'] = np.ones((1, 128), dtype=np.float32)
    c['ones_col_b'] = np.ones((128, 1), dtype=np.float32).astype(BF)
    o2 = np.zeros((128, 2), dtype=np.float32)
    o2[:64, 0] = 1.0
    o2[64:, 1] = 1.0
    c['ones2_b'] = o2.astype(BF)
    c['ones2_f'] = o2.copy()
    idl = np.zeros((128, 64), dtype=np.float32)
    idl[64:, :] = np.eye(64)
    c['identb64'] = idl.astype(BF)
    return c



class LNTmp:
    def __init__(self, kb, name):
        self.sq = kb.sb(name + "_sq", [128, D], F32)
        self.s1 = kb.sb(name + "_s1", [128, 1], F32)
        self.s2 = kb.sb(name + "_s2", [128, 1], F32)
        self.m = kb.sb(name + "_m", [128, 1], F32)
        self.v = kb.sb(name + "_v", [128, 1], F32)
        self.r = kb.sb(name + "_r", [128, 1], F32)
        self.bb = kb.sb(name + "_bb", [128, 1], F32)


def ln_tile(kb, x_ap, out_ap, t):
    kb.red(t.s1[:], x_ap, ALU.add)
    kb.act(t.sq[:], x_ap, AF.Square)
    kb.red(t.s2[:], t.sq[:], ALU.add)
    kb.ts(t.m[:], t.s1[:], 1.0 / D, ALU.mult)
    kb.ts(t.v[:], t.s2[:], 1.0 / D, ALU.mult)
    kb.stt(t.v[:], t.m[:], t.m[:], t.v[:], ALU.mult, ALU.subtract)
    kb.ts(t.v[:], t.v[:], -1.0, ALU.mult, 1e-5, ALU.add)
    kb.act(t.v[:], t.v[:], AF.Sqrt)
    kb.recip(t.r[:], t.v[:])
    kb.stt(t.bb[:], t.m[:], -1.0, t.r[:], ALU.mult, ALU.mult)
    kb.act(out_ap, x_ap, AF.Identity, bias=t.bb[:], scale=t.r[:])



def load_wpool(kb, pool_ap, dram_ap, rows, cols, stage, tag, pool_off=0):
    kc_n = rows // 128
    sw = stage.shape[1]
    for kc in range(kc_n):
        for c0 in range(0, cols, sw):
            c1 = min(cols, c0 + sw)
            kb.dma('sp', stage[:, 0:c1 - c0], dram_ap[kc * 128:(kc + 1) * 128, c0:c1])
            kb.act(pool_ap[:, pool_off + kc * cols + c0: pool_off + kc * cols + c1],
                   stage[:, 0:c1 - c0], AF.Identity)


def wblk(pool_ap, cols, kc, oc):
    return pool_ap[:, kc * cols + oc * 128: kc * cols + oc * 128 + 128]



def build_P0():
    kb = KB("P0")
    hc = host_consts()
    identb = kb.constsb('identb', hc['ident_b'])
    ones2 = kb.constsb('ones2', hc['ones2_f'])

    x = kb.din('x', [TPC, D])
    wq = kb.din('wq', [D, 3 * D])
    bq = kb.din('bq', [128, 24])

    qt = kb.dout('qt', [D, TPC], BF16)
    kt = kb.dout('kt', [D, TPC], BF16)
    vt = kb.dout('vt', [D, TPC], BF16)
    sqq = kb.dout('sqq', [HEADS, TPC], F32)
    sqk = kb.dout('sqk', [HEADS, TPC], F32)

    wpool = kb.sb('wpool', [128, 8 * 3 * D], BF16)
    wstage = kb.sb('wstage', [128, 3 * D], F32)
    load_wpool(kb, wpool, wq, D, 3 * D, wstage, 'w')
    bsb = kb.sb('bsb', [128, 24], F32)
    kb.dma('sp', bsb[:], bq)

    xs = [kb.sb(f'x{i}', [128, D], F32) for i in range(4)]
    xn = [kb.sb(f'xn{i}', [128, D], BF16) for i in range(4)]
    lnt = LNTmp(kb, 'ln')
    xnT = kb.sb('xnT', [128, 8 * 512], BF16)
    ptr = kb.ps('ptr', [128, 128], BF16)
    pmm = [kb.ps(f'pmm{i}', [128, 512], F32) for i in range(2)]
    pst = kb.ps('pst', [2, 512], F32)
    obuf = kb.sb('obuf', [128, 512], BF16)
    sqb = kb.sb('sqb', [128, 512], F32)
    stq = kb.sb('stq', [2, 512], F32)

    for st in range(NST):
        for tt in range(4):
            ti = st * 4 + tt
            kb.dma('sp', xs[tt][:], x[ti * 128:(ti + 1) * 128, :])
            ln_tile(kb, xs[tt][:], xn[tt][:], lnt)
            for fc in range(8):
                kb.transp(ptr[:], xn[tt][:, fc * 128:(fc + 1) * 128], identb[:])
                kb.dcopy(xnT[:, fc * 512 + tt * 128: fc * 512 + tt * 128 + 128], ptr[:])
        for oc in range(24):
            p = pmm[oc % 2]
            for kc in range(8):
                kb.mm(p[:], wblk(wpool, 3 * D, kc, oc), xnT[:, kc * 512:(kc + 1) * 512],
                      start=(kc == 0), stop=(kc == 7))
            kb.act(obuf[:], p[:], AF.Identity, bias=bsb[:, oc:oc + 1])
            dst = [qt, kt, vt][oc // 8]
            kb.dma('pool', dst[(oc % 8) * 128:(oc % 8) * 128 + 128, st * 512:(st + 1) * 512],
                   obuf[:], outdma=True)
            if oc < 16:
                kb.act(sqb[:], p[:], AF.Square, bias=bsb[:, oc:oc + 1])
                kb.mm(pst[:], ones2[:], sqb[:], start=True, stop=True)
                kb.dcopy(stq[:], pst[:])
                dsts = sqq if oc < 8 else sqk
                kb.dma('act', dsts[(oc % 8) * 2:(oc % 8) * 2 + 2, st * 512:(st + 1) * 512],
                       stq[:], outdma=True)
    return kb



def build_attn(tag, causal=False, scale=0.125):
    kb = KB(tag)
    hc = host_consts()
    identb = kb.constsb('identb', hc['ident_b'])
    onesrow = kb.constsb('onesrow', hc['ones_row_f'])
    if causal:
        masks = []
        for dlt in range(4):
            k_i = np.arange(128)[:, None]
            q_i = np.arange(512)[None, :]
            m = (q_i >= k_i + dlt * 128).astype(np.float32).astype(BF)
            masks.append(kb.constsb(f'cmask{dlt}', m))

    NU = UNITS_PER_CORE
    q_in = kb.din('q_in', [NU, HD, S], BF16)
    k_in = kb.din('k_in', [NU, HD, S], BF16)
    v_in = kb.din('v_in', [NU, HD, S], BF16)
    sqq_in = kb.din('sqq_in', [NU, 1, S], F32)
    sqk_in = kb.din('sqk_in', [NU, 1, S], F32)
    o_out = kb.dout('o_out', [NU, HD, S], BF16)

    KBLK = S // 128
    QST = S // 512

    qa = kb.sb('qa', [65, S], BF16)
    ka = kb.sb('ka', [65, S], BF16)
    vtk = kb.sb('vtk', [128, 65 * KBLK], BF16)
    vts = kb.sb('vts', [64, S], BF16)
    nq = kb.sb('nq', [1, S], F32)
    nk = kb.sb('nk', [1, S], F32)
    mk = kb.sb('mk', [1, 1], F32)
    pS = [kb.ps(f'pS{i}', [128, 512], F32) for i in range(2)]
    pV = [kb.ps(f'pV{i}', [65, 512], F32) for i in range(2)]
    ptr = kb.ps('ptr', [128, 128], BF16)
    pbc = kb.ps('pbc', [64, 512], F32)
    pexp = kb.sb('pexp', [128, 8 * 512], BF16)
    oraw = kb.sb('oraw', [65, 512], F32)
    den = kb.sb('den', [1, 512], F32)
    obf = kb.sb('obf', [64, 512], BF16)
    augrow = kb.sb('augrow', [1, S], BF16)

    for u in range(NU):
        kb.dma('sp', qa[0:64, :], q_in[u])
        kb.dma('sp', ka[0:64, :], k_in[u])
        kb.dma('sp', vts[:], v_in[u])
        kb.dma('sp', nq[:], sqq_in[u])
        kb.dma('sp', nk[:], sqk_in[u])
        kb.act(nq[:], nq[:], AF.Sqrt)
        kb.act(nk[:], nk[:], AF.Sqrt)
        kb.red(mk[:], nk[:], ALU.max)
        kb.ts(augrow[:], nq[:], mk[:], ALU.mult, -1.0, ALU.mult)
        kb.dcopy(qa[64:65, :], augrow[:])
        kb.memset(ka[64:65, :], 1.0)
        kb.memset(vtk[:], 1.0)
        for j in range(KBLK):
            kb.transp(ptr[0:128, 0:64], vts[:, j * 128:(j + 1) * 128], identb[:])
            kb.dcopy(vtk[:, j * 65: j * 65 + 64], ptr[0:128, 0:64])

        for qs in range(QST):
            jmax = KBLK if not causal else min(KBLK, (qs + 1) * 4)
            for j in range(jmax):
                p = pS[j % 2]
                kb.mm(p[:], ka[:, j * 128:(j + 1) * 128], qa[:, qs * 512:(qs + 1) * 512],
                      start=True, stop=True)
                kb.act(pexp[:, (j % 8) * 512:(j % 8) * 512 + 512], p[:], AF.Exp, scale=scale)
                if causal and j >= qs * 4:
                    dlt = j - qs * 4
                    kb.tt(pexp[:, (j % 8) * 512:(j % 8) * 512 + 512],
                          pexp[:, (j % 8) * 512:(j % 8) * 512 + 512],
                          masks[dlt][:], ALU.mult)
                pv = pV[qs % 2]
                kb.mm(pv[:], vtk[:, j * 65:(j + 1) * 65],
                      pexp[:, (j % 8) * 512:(j % 8) * 512 + 512],
                      start=(j == 0), stop=(j == jmax - 1))
            pv = pV[qs % 2]
            kb.act(oraw[:], pv[:], AF.Identity)
            kb.recip(den[:], oraw[64:65, :])
            kb.mm(pbc[:], onesrow[0:1, 0:64], den[:], start=True, stop=True)
            kb.tt(obf[:], oraw[0:64, :], pbc[:], ALU.mult)
            kb.dma('pool', o_out[u][:, qs * 512:(qs + 1) * 512], obf[:], outdma=True)
    return kb



def build_attn_q(tag, scale=1.0 / 32.0):
    kb = KB(tag)
    hc = host_consts()
    identb = kb.constsb('identb', hc['ident_b'])

    NU = UNITS_PER_CORE
    q_in = kb.din('q_in', [NU, HD, S], BF16)
    k_in = kb.din('k_in', [NU, HD, S], BF16)
    v_in = kb.din('v_in', [NU, HD, S], BF16)
    sqq_in = kb.din('sqq_in', [NU, 1, S], F32)
    sqk_in = kb.din('sqk_in', [NU, 1, S], F32)
    o_out = kb.dout('o_out', [NU, HD, S], BF16)

    KBLK = S // 128
    QST = S // 512

    qa = kb.sb('qa', [65, S], BF16)
    ka = kb.sb('ka', [65, S], BF16)
    vts = kb.sb('vts', [64, S], BF16)
    nq = kb.sb('nq', [1, S], F32)
    nk = kb.sb('nk', [1, S], F32)
    mq = kb.sb('mq', [1, 1], F32)
    augrow = kb.sb('augrow', [1, S], BF16)
    pS = [kb.ps(f'pS{i}', [128, 512], F32) for i in range(2)]
    pV = [kb.ps(f'pV{i}', [64, 512], F32) for i in range(2)]
    ptr = kb.ps('ptr', [128, 128], BF16)
    pexp = kb.sb('pexp', [128, S], BF16)
    zacc = kb.sb('zacc', [128, 16], F32)
    z = kb.sb('z', [128, 1], F32)
    zr = kb.sb('zr', [128, 1], F32)
    vtk = kb.sb('vtk', [128, 64], BF16)
    vtks = kb.sb('vtks', [128, 64], BF16)
    oacc = kb.sb('oacc', [64, S], F32)
    obf = kb.sb('obf', [64, 512], BF16)

    for u in range(NU):
        kb.dma('sp', qa[0:64, :], q_in[u])
        kb.dma('sp', ka[0:64, :], k_in[u])
        kb.dma('sp', vts[:], v_in[u])
        kb.dma('sp', nq[:], sqq_in[u])
        kb.dma('sp', nk[:], sqk_in[u])
        kb.act(nq[:], nq[:], AF.Sqrt)
        kb.act(nk[:], nk[:], AF.Sqrt)
        kb.red(mq[:], nq[:], ALU.max)
        kb.ts(augrow[:], nk[:], mq[:], ALU.mult, -1.0, ALU.mult)
        kb.dcopy(ka[64:65, :], augrow[:])
        kb.memset(qa[64:65, :], 1.0)
        kb.memset(oacc[:], 0.0)

        for j in range(KBLK):
            for qs in range(QST):
                p = pS[qs % 2]
                kb.mm(p[:], ka[:, j * 128:(j + 1) * 128], qa[:, qs * 512:(qs + 1) * 512],
                      start=True, stop=True)
                kb.act(pexp[:, qs * 512:(qs + 1) * 512], p[:], AF.Exp, scale=scale,
                       accum_out=zacc[:, qs:qs + 1])
            kb.red(z[:], zacc[:], ALU.add)
            kb.recip(zr[:], z[:])
            kb.transp(ptr[0:128, 0:64], vts[:, j * 128:(j + 1) * 128], identb[:])
            kb.dcopy(vtk[:], ptr[0:128, 0:64])
            kb.act(vtks[:], vtk[:], AF.Identity, scale=zr[:])
            for qs in range(QST):
                pv = pV[qs % 2]
                kb.mm(pv[:], vtks[:], pexp[:, qs * 512:(qs + 1) * 512], start=True, stop=True)
                kb.tt(oacc[:, qs * 512:(qs + 1) * 512], oacc[:, qs * 512:(qs + 1) * 512],
                      pv[:], ALU.add)
        for qs in range(QST):
            kb.dcopy(obf[:], oacc[:, qs * 512:(qs + 1) * 512])
            kb.dma('pool', o_out[u][:, qs * 512:(qs + 1) * 512], obf[:], outdma=True)
    return kb



def shard_tokens(a):
    return [np.ascontiguousarray(a[c * TPC:(c + 1) * TPC]) for c in range(NCORES)]


def assemble_heads(per_core_fmaj, name):
    full = [np.concatenate([per_core_fmaj[b * 4 + i] for i in range(4)], axis=1) for b in range(B)]
    per_core = []
    for c in range(NCORES):
        us = []
        for uu in range(UNITS_PER_CORE):
            unit = c * UNITS_PER_CORE + uu
            b, h = unit // HEADS, unit % HEADS
            us.append(full[b][h * 64:(h + 1) * 64, :])
        per_core.append(np.ascontiguousarray(np.stack(us)))
    return per_core


def assemble_norm(per_core_sq, name):
    full = [np.concatenate([per_core_sq[b * 4 + i] for i in range(4)], axis=1) for b in range(B)]
    per_core = []
    for c in range(NCORES):
        us = []
        for uu in range(UNITS_PER_CORE):
            unit = c * UNITS_PER_CORE + uu
            b, h = unit // HEADS, unit % HEADS
            us.append(full[b][h:h + 1, :])
        per_core.append(np.ascontiguousarray(np.stack(us)))
    return per_core


def disassemble_o(res_list):
    full = {}
    for c in range(NCORES):
        for uu in range(UNITS_PER_CORE):
            unit = c * UNITS_PER_CORE + uu
            b, h = unit // HEADS, unit % HEADS
            full.setdefault(b, {})[h] = res_list[c][uu]
    fm = [np.concatenate([full[b][h] for h in range(HEADS)], axis=0) for b in range(B)]
    out = []
    for c in range(NCORES):
        b = c // 4
        s0 = (c % 4) * TPC
        out.append(np.ascontiguousarray(fm[b][:, s0:s0 + TPC]))
    return out




def _epilogue_wo(kb, st, otb, wpool, wcols, bias_sb, xs, ysb, pmm, ptr_f, identf, xt_in):
    for oc in range(8):
        p = pmm[oc % 2]
        for kc in range(8):
            kb.mm(p[:], wblk(wpool, wcols, kc, oc), otb[:, kc * 512:(kc + 1) * 512],
                  start=(kc == 0), stop=(kc == 7))
        if bias_sb is not None:
            kb.act(ysb[:], p[:], AF.Identity, bias=bias_sb[:, oc:oc + 1])
        else:
            kb.act(ysb[:], p[:], AF.Identity)
        for tt in range(4):
            kb.transp(ptr_f[:], ysb[:, tt * 128:(tt + 1) * 128], identf[:])
            kb.tt(xs[tt][:, oc * 128:(oc + 1) * 128], xs[tt][:, oc * 128:(oc + 1) * 128],
                  ptr_f[:], ALU.add)


def _ln_xnt(kb, st, xs, xn, lnt, xnT, ptr_b, identb):
    for tt in range(4):
        ln_tile(kb, xs[tt][:], xn[tt][:], lnt)
        for fc in range(8):
            kb.transp(ptr_b[:], xn[tt][:, fc * 128:(fc + 1) * 128], identb[:])
            kb.dcopy(xnT[:, fc * 512 + tt * 128: fc * 512 + tt * 128 + 128], ptr_b[:])



def build_E0():
    kb = KB("E0")
    hc = host_consts()
    identb = kb.constsb('identb', hc['ident_b'])
    identf = kb.constsb('identf', hc['ident_f'])
    identb64 = kb.constsb('identb64', hc['identb64'])
    onesrowf = kb.constsb('onesrowf', hc['ones_row_f'])
    iota8 = kb.constsb('iota8', np.broadcast_to(np.arange(8, dtype=np.float32), (128, 8)).copy())

    x = kb.din('x', [TPC, D])
    ot = kb.din('ot', [D, TPC], BF16)
    woT = kb.din('woT', [D, D])
    bo = kb.din('bo', [128, 8])
    gwT = kb.din('gwT', [D, 8])
    gb = kb.din('gb', [1, 8])
    w1T = kb.din('w1T', [8, D, 4096])
    w2T = kb.din('w2T', [8, 4096, D])
    b1h = kb.din('b1h', [8, 128, 32])
    b2h = kb.din('b2h', [8, 128, 8])
    wq1 = kb.din('wq1', [D, 3 * D])
    bq1 = kb.din('bq1', [128, 24])

    xo = kb.dout('xo', [TPC, D])
    x2o = kb.dout('x2o', [TPC, D])
    q1t = kb.dout('q1t', [D, TPC], BF16)
    kvo = kb.dout('kvo', [128, 8 * 64])
    kso = kb.dout('kso', [128, 8])

    pool = kb.sb('pool', [128, 32768], BF16)
    wosb = kb.sb('wosb', [128, 8192], BF16)
    stage = kb.sb('stage', [128, 1024], F32)
    hsb = kb.sb('hsb', [128, 32 * 512], BF16)
    xnT = kb.sb('xnT', [128, 8 * 512], BF16)
    xtl = kb.sb('xtl', [128, 8 * 512], BF16)
    macc = kb.sb('macc', [128, 8 * 512], F32)
    xs = [kb.sb(f'x{i}', [128, D], F32) for i in range(4)]
    xn = [kb.sb(f'xn{i}', [128, D], BF16) for i in range(4)]
    lnt = LNTmp(kb, 'ln')
    ysb = kb.sb('ysb', [128, 512], F32)
    bosb = kb.sb('bosb', [128, 8], F32)
    gbb = kb.sb('gbb', [128, 8], F32)
    gwsb = kb.sb('gwsb', [128, 8 * 8], BF16)
    logit = kb.sb('logit', [128, 8], F32)
    mrow = kb.sb('mrow', [128, 1], F32)
    nmrow = kb.sb('nmrow', [128, 1], F32)
    zrow = kb.sb('zrow', [128, 1], F32)
    zre = kb.sb('zre', [128, 1], F32)
    prob = kb.sb('prob', [128, 8], F32)
    eq1 = kb.sb('eq1', [128, 8], F32)
    tmp8 = kb.sb('tmp8', [128, 8], F32)
    i1 = kb.sb('i1', [128, 1], F32)
    i2 = kb.sb('i2', [128, 1], F32)
    chosen = kb.sb('chosen', [128, 1], F32)
    ohsel = kb.sb('ohsel', [128, 8], F32)
    ssel = kb.sb('ssel', [128, 1], F32)
    scale_sb = kb.sb('scale_sb', [128, 16 * 8], F32)
    mask_sb = kb.sb('mask_sb', [128, 16 * 8], F32)
    scr = kb.sb('scr', [1, 512], F32)
    mcr = kb.sb('mcr', [1, 512], F32)
    b1sb = kb.sb('b1sb', [128, 32], F32)
    b2sb = kb.sb('b2sb', [128, 8], F32)
    ytmp = kb.sb('ytmp', [128, 512], F32)
    bq1sb = kb.sb('bq1sb', [128, 24], F32)
    zb = kb.sb('zb', [128, 512], F32)
    arel = kb.sb('arel', [128, 512], F32)
    aexp = kb.sb('aexp', [128, 512], F32)
    qkvb = kb.sb('qkvb', [128, 512], BF16)
    ksb = kb.sb('ksb', [128, 8 * 512], BF16)
    ktok = kb.sb('ktok', [128, 64], BF16)
    vtok = kb.sb('vtok', [128, 64], BF16)
    kvacc = kb.sb('kvacc', [128, 8 * 64], F32)
    ksacc = kb.sb('ksacc', [128, 8], F32)
    red1 = kb.sb('red1', [128, 1], F32)

    pmm = [kb.ps(f'pmm{i}', [128, 512], F32) for i in range(2)]
    ptr_b = kb.ps('ptr_b', [128, 128], BF16)
    ptr_f = kb.ps('ptr_f', [128, 128], F32)
    pbc1 = kb.ps('pbc1', [128, 512], F32)
    pbc2 = kb.ps('pbc2', [128, 512], F32)
    pkv = kb.ps('pkv', [64, 64], F32)
    pr1 = kb.ps('pr1', [1, 512], F32)

    load_wpool(kb, wosb, woT, D, D, stage, 'wo')
    kb.dma('sp', bosb[:], bo)
    kb.dma('sp', bq1sb[:], bq1)
    for kc in range(8):
        kb.dma('sp', stage[:, 0:8], gwT[kc * 128:(kc + 1) * 128, :])
        kb.act(gwsb[:, kc * 8:(kc + 1) * 8], stage[:, 0:8], AF.Identity)
    kb.dma('sp', stage[0:1, 0:8], gb)
    kb.mm(pbc1[:, 0:8], onesrowf[:], stage[0:1, 0:8], start=True, stop=True)
    kb.dcopy(gbb[:], pbc1[:, 0:8])
    kb.memset(kvacc[:], 0.0)
    kb.memset(ksacc[:], 0.0)

    for st in range(NST):
        for tt in range(4):
            ti = st * 4 + tt
            kb.dma('sp', xs[tt][:], x[ti * 128:(ti + 1) * 128, :])
        for kc in range(8):
            kb.dma('sp', xtl[:, kc * 512:(kc + 1) * 512],
                   ot[kc * 128:(kc + 1) * 128, st * 512:(st + 1) * 512])
        _epilogue_wo(kb, st, xtl, wosb, D, bosb, xs, ysb, pmm, ptr_f, identf, x)
        for tt in range(4):
            ti = st * 4 + tt
            kb.dma('pool', x2o[ti * 128:(ti + 1) * 128, :], xs[tt][:], outdma=True)
        _ln_xnt(kb, st, xs, xn, lnt, xnT, ptr_b, identb)
        for tt in range(4):
            ti = st * 4 + tt
            p = pmm[0]
            for kc in range(8):
                kb.mm(p[:, 0:8], xnT[:, kc * 512 + tt * 128: kc * 512 + tt * 128 + 128],
                      gwsb[:, kc * 8:(kc + 1) * 8], start=(kc == 0), stop=(kc == 7))
            kb.dcopy(logit[:], p[:, 0:8])
            kb.tt(logit[:], logit[:], gbb[:], ALU.add)
            kb.red(mrow[:], logit[:], ALU.max)
            kb.ts(nmrow[:], mrow[:], -1.0, ALU.mult)
            kb.act(prob[:], logit[:], AF.Exp, bias=nmrow[:], accum_out=zrow[:])
            kb.recip(zre[:], zrow[:])
            kb.act(prob[:], prob[:], AF.Identity, scale=zre[:])
            kb.red(mrow[:], prob[:], ALU.max)
            kb.ts(eq1[:], prob[:], mrow[:], ALU.is_equal)
            kb.tt(tmp8[:], eq1[:], iota8[:], ALU.mult)
            kb.red(i1[:], tmp8[:], ALU.max)
            kb.stt(tmp8[:], eq1[:], -1e9, prob[:], ALU.mult, ALU.add)
            kb.red(mrow[:], tmp8[:], ALU.max)
            kb.ts(eq1[:], tmp8[:], mrow[:], ALU.is_equal)
            kb.tt(tmp8[:], eq1[:], iota8[:], ALU.mult)
            kb.red(i2[:], tmp8[:], ALU.max)
            kb.tt(chosen[:], i1[:], i2[:], ALU.max)
            kb.ts(ohsel[:], iota8[:], chosen[:], ALU.is_equal)
            kb.tt(tmp8[:], ohsel[:], prob[:], ALU.mult)
            kb.red(ssel[:], tmp8[:], ALU.add)
            for e in range(8):
                kb.ts(scale_sb[:, e * 16 + ti:e * 16 + ti + 1], ohsel[:, e:e + 1],
                      ssel[:], ALU.mult)
                kb.dcopy(mask_sb[:, e * 16 + ti:e * 16 + ti + 1], ohsel[:, e:e + 1])
        kb.memset(macc[:], 0.0)
        for e in range(8):
            for tt in range(4):
                ti = st * 4 + tt
                kb.transp(pr1[0:1, tt * 128:(tt + 1) * 128],
                          scale_sb[:, e * 16 + ti:e * 16 + ti + 1], identf[:])
            kb.dcopy(scr[:], pr1[:])
            for tt in range(4):
                ti = st * 4 + tt
                kb.transp(pr1[0:1, tt * 128:(tt + 1) * 128],
                          mask_sb[:, e * 16 + ti:e * 16 + ti + 1], identf[:])
            kb.dcopy(mcr[:], pr1[:])
            kb.mm(pbc1[:], onesrowf[:], scr[:], start=True, stop=True)
            kb.mm(pbc2[:], onesrowf[:], mcr[:], start=True, stop=True)
            for kc in range(8):
                kb.tt(xtl[:, kc * 512:(kc + 1) * 512], xnT[:, kc * 512:(kc + 1) * 512],
                      pbc1[:], ALU.mult)
            load_wpool(kb, pool, w1T[e], D, 4096, stage, 'w1')
            kb.dma('sp', b1sb[:], b1h[e])
            kb.dma('sp', b2sb[:], b2h[e])
            for oc in range(32):
                p = pmm[oc % 2]
                for kc in range(8):
                    kb.mm(p[:], wblk(pool, 4096, kc, oc),
                          xtl[:, kc * 512:(kc + 1) * 512], start=(kc == 0), stop=(kc == 7))
                kb.act(hsb[:, oc * 512:(oc + 1) * 512], p[:], AF.Relu, bias=b1sb[:, oc:oc + 1])
            load_wpool(kb, pool, w2T[e], 4096, 1024, stage, 'w2')
            for oc in range(8):
                p = pmm[oc % 2]
                for kc in range(32):
                    kb.mm(p[:], wblk(pool, 1024, kc, oc),
                          hsb[:, kc * 512:(kc + 1) * 512], start=(kc == 0), stop=(kc == 31))
                kb.act(ytmp[:], p[:], AF.Identity, bias=b2sb[:, oc:oc + 1])
                kb.tt(ytmp[:], ytmp[:], pbc2[:], ALU.mult)
                kb.tt(macc[:, oc * 512:(oc + 1) * 512], macc[:, oc * 512:(oc + 1) * 512],
                      ytmp[:], ALU.add)
        for oc in range(8):
            for tt in range(4):
                kb.transp(ptr_f[:], macc[:, oc * 512 + tt * 128: oc * 512 + tt * 128 + 128],
                          identf[:])
                kb.tt(xs[tt][:, oc * 128:(oc + 1) * 128], xs[tt][:, oc * 128:(oc + 1) * 128],
                      ptr_f[:], ALU.add)
        for tt in range(4):
            ti = st * 4 + tt
            kb.dma('pool', xo[ti * 128:(ti + 1) * 128, :], xs[tt][:], outdma=True)
        _ln_xnt(kb, st, xs, xn, lnt, xnT, ptr_b, identb)
        load_wpool(kb, pool, wq1, D, 3 * D, stage, 'wq1')
        for oc in range(24):
            p = pmm[oc % 2]
            for kc in range(8):
                kb.mm(p[:], wblk(pool, 3 * D, kc, oc), xnT[:, kc * 512:(kc + 1) * 512],
                      start=(kc == 0), stop=(kc == 7))
            kb.act(zb[:], p[:], AF.Identity, bias=bq1sb[:, oc:oc + 1])
            if oc < 16:
                kb.act(arel[:], zb[:], AF.Relu)
                kb.tss(ysb[:], zb[:], 0.0, ALU.min)
                kb.act(aexp[:], ysb[:], AF.Exp)
                kb.tt(qkvb[:], arel[:], aexp[:], ALU.add)
            else:
                kb.dcopy(qkvb[:], zb[:])
            if oc < 8:
                kb.dma('pool', q1t[oc * 128:(oc + 1) * 128, st * 512:(st + 1) * 512],
                       qkvb[:], outdma=True)
            elif oc < 16:
                kb.dcopy(ksb[:, (oc - 8) * 512:(oc - 7) * 512], qkvb[:])
                kb.red(red1[:], qkvb[:], ALU.add)
                kb.tt(ksacc[:, oc - 8:oc - 7], ksacc[:, oc - 8:oc - 7], red1[:], ALU.add)
            else:
                ch = oc - 16
                for hh in range(2):
                    h = ch * 2 + hh
                    pb = hh * 64
                    idh = identb[:] if pb == 0 else identb64[64:128, :]
                    for tt in range(4):
                        kb.transp(ptr_b[0:128, 0:64],
                                  ksb[pb:pb + 64, ch * 512 + tt * 128: ch * 512 + tt * 128 + 128],
                                  idh)
                        kb.dcopy(ktok[:], ptr_b[0:128, 0:64])
                        kb.transp(ptr_b[0:128, 0:64],
                                  qkvb[pb:pb + 64, tt * 128:(tt + 1) * 128],
                                  idh)
                        kb.dcopy(vtok[:], ptr_b[0:128, 0:64])
                        kb.mm(pkv[:], ktok[:], vtok[:], start=True, stop=True)
                        kvs = kvacc[(h % 2) * 64:(h % 2) * 64 + 64, (h // 2) * 64:(h // 2) * 64 + 64]
                        kb.tt(kvs, kvs, pkv[:], ALU.add)
    kb.dma('pool', kvo, kvacc[:], outdma=True)
    kb.dma('pool', kso, ksacc[:], outdma=True)
    return kb


def build_E1():
    kb = KB("E1")
    hc = host_consts()
    identb = kb.constsb('identb', hc['ident_b'])
    identf = kb.constsb('identf', hc['ident_f'])
    onesrowf = kb.constsb('onesrowf', hc['ones_row_f'])
    ones2 = kb.constsb('ones2', hc['ones2_f'])

    x = kb.din('x', [TPC, D])
    q1t = kb.din('q1t', [D, TPC], BF16)
    kvp = kb.din('kvp', [4, 128, 512])
    ksp = kb.din('ksp', [4, 128, 8])
    stdp = kb.din('stdp', [1, 16])
    wo1T = kb.din('wo1T', [D, D])
    bo1 = kb.din('bo1', [128, 8])
    wqkv2 = kb.din('wqkv2', [D, 3 * D])

    xo = kb.dout('xo', [TPC, D])
    q2t = kb.dout('q2t', [D, TPC], BF16)
    k2t = kb.dout('k2t', [D, TPC], BF16)
    v2t = kb.dout('v2t', [D, TPC], BF16)
    sqq2 = kb.dout('sqq2', [HEADS, TPC], F32)
    sqk2 = kb.dout('sqk2', [HEADS, TPC], F32)

    pool = kb.sb('pool', [128, 8192 + 24576], BF16)
    stage = kb.sb('stage', [128, 1024], F32)
    xs = [kb.sb(f'x{i}', [128, D], F32) for i in range(4)]
    xn = [kb.sb(f'xn{i}', [128, D], BF16) for i in range(4)]
    lnt = LNTmp(kb, 'ln')
    xnT = kb.sb('xnT', [128, 8 * 512], BF16)
    qsb = kb.sb('qsb', [128, 8 * 512], BF16)
    o1sb = kb.sb('o1sb', [128, 8 * 512], BF16)
    ysb = kb.sb('ysb', [128, 512], F32)
    bosb = kb.sb('bosb', [128, 8], F32)
    kvt = kb.sb('kvt', [128, 512], F32)
    kvf = kb.sb('kvf', [128, 512], F32)
    kvb = kb.sb('kvb', [128, 512], BF16)
    kst = kb.sb('kst', [128, 8], F32)
    ksf = kb.sb('ksf', [128, 8], F32)
    ksb = kb.sb('ksb', [128, 8], BF16)
    sg = kb.sb('sg', [1, 16], F32)
    drow = kb.sb('drow', [1, 512], F32)
    numsb = kb.sb('numsb', [64, 512], F32)
    qkvb = kb.sb('qkvb', [128, 512], BF16)
    sqb = kb.sb('sqb', [128, 512], F32)
    stq = kb.sb('stq', [2, 512], F32)

    pmm = [kb.ps(f'pmm{i}', [128, 512], F32) for i in range(2)]
    pnum = kb.ps('pnum', [64, 512], F32)
    pden = kb.ps('pden', [1, 512], F32)
    pbc = kb.ps('pbc', [64, 512], F32)
    ptr_b = kb.ps('ptr_b', [128, 128], BF16)
    ptr_f = kb.ps('ptr_f', [128, 128], F32)
    pst = kb.ps('pst', [2, 512], F32)

    kb.dma('sp', kvf[:], kvp[0])
    for i in range(1, 4):
        kb.dma('sp', kvt[:], kvp[i])
        kb.tt(kvf[:], kvf[:], kvt[:], ALU.add)
    kb.dcopy(kvb[:], kvf[:])
    kb.dma('sp', ksf[:], ksp[0])
    for i in range(1, 4):
        kb.dma('sp', kst[:], ksp[i])
        kb.tt(ksf[:], ksf[:], kst[:], ALU.add)
    kb.dcopy(ksb[:], ksf[:])
    kb.dma('sp', sg[:], stdp)
    kb.act(sg[:], sg[:], AF.Sigmoid)
    load_wpool(kb, pool, wo1T, D, D, stage, 'wo1')
    kb.dma('sp', bosb[:], bo1)

    for st in range(NST):
        for tt in range(4):
            ti = st * 4 + tt
            kb.dma('sp', xs[tt][:], x[ti * 128:(ti + 1) * 128, :])
        for kc in range(8):
            kb.dma('sp', qsb[:, kc * 512:(kc + 1) * 512],
                   q1t[kc * 128:(kc + 1) * 128, st * 512:(st + 1) * 512])
        for h in range(HEADS):
            pb = (h % 2) * 64
            ch = h // 2
            qh = qsb[pb:pb + 64, ch * 512:(ch + 1) * 512]
            kvh = kvb[pb:pb + 64, (h // 2) * 64:(h // 2) * 64 + 64]
            ksh = ksb[pb:pb + 64, h // 2:h // 2 + 1]
            kb.mm(pnum[:], kvh, qh, start=True, stop=True)
            kb.mm(pden[:], ksh, qh, start=True, stop=True)
            kb.ts(drow[:], pden[:], 1e-6, ALU.add)
            kb.recip(drow[:], drow[:])
            kb.ts(drow[:], drow[:], sg[0:1, h:h + 1], ALU.mult)
            kb.mm(pbc[:], onesrowf[0:1, 0:64], drow[:], start=True, stop=True)
            kb.act(numsb[:], pnum[:], AF.Identity)
            kb.tt(o1sb[pb:pb + 64, ch * 512:(ch + 1) * 512], numsb[:], pbc[:], ALU.mult)
        _epilogue_wo(kb, st, o1sb, pool, D, bosb, xs, ysb, pmm, ptr_f, identf, x)
        for tt in range(4):
            ti = st * 4 + tt
            kb.dma('pool', xo[ti * 128:(ti + 1) * 128, :], xs[tt][:], outdma=True)
        _ln_xnt(kb, st, xs, xn, lnt, xnT, ptr_b, identb)
        if st == 0:
            load_wpool(kb, pool, wqkv2, D, 3 * D, stage, 'wqkv2', pool_off=8192)
        for oc in range(24):
            p = pmm[oc % 2]
            for kc in range(8):
                kb.mm(p[:], pool[:, 8192 + kc * 3 * D + oc * 128: 8192 + kc * 3 * D + oc * 128 + 128],
                      xnT[:, kc * 512:(kc + 1) * 512], start=(kc == 0), stop=(kc == 7))
            kb.act(qkvb[:], p[:], AF.Identity)
            dst = [q2t, k2t, v2t][oc // 8]
            kb.dma('pool', dst[(oc % 8) * 128:(oc % 8) * 128 + 128, st * 512:(st + 1) * 512],
                   qkvb[:], outdma=True)
            if oc < 16:
                kb.act(sqb[:], p[:], AF.Square)
                kb.mm(pst[:], ones2[:], sqb[:], start=True, stop=True)
                kb.dcopy(stq[:], pst[:])
                dsts = sqq2 if oc < 8 else sqk2
                kb.dma('act', dsts[(oc % 8) * 2:(oc % 8) * 2 + 2, st * 512:(st + 1) * 512],
                       stq[:], outdma=True)
    return kb



def build_E2():
    kb = KB("E2")
    hc = host_consts()
    identb = kb.constsb('identb', hc['ident_b'])
    identf = kb.constsb('identf', hc['ident_f'])
    onesrowf = kb.constsb('onesrowf', hc['ones_row_f'])
    onescolf = kb.constsb('onescolf', np.ones((128, 1), dtype=np.float32))
    ones2 = kb.constsb('ones2', hc['ones2_f'])

    x = kb.din('x', [TPC, D])
    ot = kb.din('ot', [D, TPC], BF16)
    wo2 = kb.din('wo2', [D, D])
    wff = kb.din('wff', [D, D])
    wqkv3 = kb.din('wqkv3', [D, 3 * D])
    bqkv3 = kb.din('bqkv3', [128, 24])

    xo = kb.dout('xo', [TPC, D])
    q3t = kb.dout('q3t', [D, TPC], BF16)
    k3t = kb.dout('k3t', [D, TPC], BF16)
    v3t = kb.dout('v3t', [D, TPC], BF16)
    sqq3 = kb.dout('sqq3', [HEADS, TPC], F32)
    sqk3 = kb.dout('sqk3', [HEADS, TPC], F32)

    pool = kb.sb('pool', [128, 8192 + 8192 + 24576], BF16)
    stage = kb.sb('stage', [128, 1024], F32)
    xs = [kb.sb(f'x{i}', [128, D], F32) for i in range(4)]
    xn = [kb.sb(f'xn{i}', [128, D], BF16) for i in range(4)]
    lnt = LNTmp(kb, 'ln')
    xnT = kb.sb('xnT', [128, 8 * 512], BF16)
    otb = kb.sb('otb', [128, 8 * 512], BF16)
    zsb = kb.sb('zsb', [128, 8 * 512], F32)
    znb = kb.sb('znb', [128, 8 * 512], BF16)
    sqch = kb.sb('sqch', [128, 512], F32)
    nrow = kb.sb('nrow', [1, 512], F32)
    zzf = kb.sb('zzf', [128, 512], F32)
    gf = kb.sb('gf', [128, 512], F32)
    qkvb = kb.sb('qkvb', [128, 512], BF16)
    sqb = kb.sb('sqb', [128, 512], F32)
    stq = kb.sb('stq', [2, 512], F32)
    bsb = kb.sb('bsb', [128, 24], F32)

    pmm = [kb.ps(f'pmm{i}', [128, 512], F32) for i in range(2)]
    pnrm = kb.ps('pnrm', [1, 512], F32)
    pbc = kb.ps('pbc', [128, 512], F32)
    ptr_b = kb.ps('ptr_b', [128, 128], BF16)
    ptr_f = kb.ps('ptr_f', [128, 128], F32)
    pst = kb.ps('pst', [2, 512], F32)

    load_wpool(kb, pool, wo2, D, D, stage, 'wo2')
    load_wpool(kb, pool, wff, D, D, stage, 'wff', pool_off=8192)
    load_wpool(kb, pool, wqkv3, D, 3 * D, stage, 'wqkv3', pool_off=16384)
    kb.dma('sp', bsb[:], bqkv3)

    def l2norm_chunks(src, dstbf):
        for kc in range(8):
            kb.act(sqch[:], src[:, kc * 512:(kc + 1) * 512], AF.Square)
            kb.mm(pnrm[:], onescolf[:], sqch[:], start=(kc == 0), stop=(kc == 7))
        kb.act(nrow[:], pnrm[:], AF.Sqrt)
        kb.tss(nrow[:], nrow[:], 1e-12, ALU.max)
        kb.recip(nrow[:], nrow[:])
        kb.mm(pbc[:], onesrowf[:], nrow[:], start=True, stop=True)

    for st in range(NST):
        for tt in range(4):
            ti = st * 4 + tt
            kb.dma('sp', xs[tt][:], x[ti * 128:(ti + 1) * 128, :])
        for kc in range(8):
            kb.dma('sp', otb[:, kc * 512:(kc + 1) * 512],
                   ot[kc * 128:(kc + 1) * 128, st * 512:(st + 1) * 512])
        for oc in range(8):
            p = pmm[oc % 2]
            for kc in range(8):
                kb.mm(p[:], wblk(pool, D, kc, oc), otb[:, kc * 512:(kc + 1) * 512],
                      start=(kc == 0), stop=(kc == 7))
            kb.act(zsb[:, oc * 512:(oc + 1) * 512], p[:], AF.Identity)
        l2norm_chunks(zsb, True)
        for kc in range(8):
            kb.tt(znb[:, kc * 512:(kc + 1) * 512], zsb[:, kc * 512:(kc + 1) * 512],
                  pbc[:], ALU.mult)
        for oc in range(8):
            p = pmm[oc % 2]
            for kc in range(8):
                kb.mm(p[:], pool[:, 8192 + kc * D + oc * 128: 8192 + kc * D + oc * 128 + 128],
                      znb[:, kc * 512:(kc + 1) * 512], start=(kc == 0), stop=(kc == 7))
            kb.act(zsb[:, oc * 512:(oc + 1) * 512], p[:], AF.Identity)
        l2norm_chunks(zsb, False)
        for kc in range(8):
            kb.tt(zzf[:], zsb[:, kc * 512:(kc + 1) * 512], pbc[:], ALU.mult)
            kb.act(gf[:], zzf[:], AF.Gelu)
            for tt in range(4):
                kb.transp(ptr_f[:], gf[:, tt * 128:(tt + 1) * 128], identf[:])
                kb.tt(xs[tt][:, kc * 128:(kc + 1) * 128], xs[tt][:, kc * 128:(kc + 1) * 128],
                      ptr_f[:], ALU.add)
        for tt in range(4):
            ti = st * 4 + tt
            kb.dma('pool', xo[ti * 128:(ti + 1) * 128, :], xs[tt][:], outdma=True)
        _ln_xnt(kb, st, xs, xn, lnt, xnT, ptr_b, identb)
        for oc in range(24):
            p = pmm[oc % 2]
            for kc in range(8):
                kb.mm(p[:], pool[:, 16384 + kc * 3 * D + oc * 128: 16384 + kc * 3 * D + oc * 128 + 128],
                      xnT[:, kc * 512:(kc + 1) * 512], start=(kc == 0), stop=(kc == 7))
            kb.act(qkvb[:], p[:], AF.Identity, bias=bsb[:, oc:oc + 1])
            dst = [q3t, k3t, v3t][oc // 8]
            kb.dma('pool', dst[(oc % 8) * 128:(oc % 8) * 128 + 128, st * 512:(st + 1) * 512],
                   qkvb[:], outdma=True)
            if oc < 16:
                kb.act(sqb[:], p[:], AF.Square, bias=bsb[:, oc:oc + 1])
                kb.mm(pst[:], ones2[:], sqb[:], start=True, stop=True)
                kb.dcopy(stq[:], pst[:])
                dsts = sqq3 if oc < 8 else sqk3
                kb.dma('act', dsts[(oc % 8) * 2:(oc % 8) * 2 + 2, st * 512:(st + 1) * 512],
                       stq[:], outdma=True)
    return kb


def build_E3():
    kb = KB("E3")
    hc = host_consts()
    identb = kb.constsb('identb', hc['ident_b'])
    identf = kb.constsb('identf', hc['ident_f'])
    onesrowf = kb.constsb('onesrowf', hc['ones_row_f'])
    ones2 = kb.constsb('ones2', hc['ones2_f'])

    x = kb.din('x', [TPC, D])
    ot = kb.din('ot', [D, TPC], BF16)
    wo3T = kb.din('wo3T', [D, D])
    bo3 = kb.din('bo3', [128, 8])
    w_fc = kb.din('w_fc', [4096, D])
    w_proj = kb.din('w_proj', [D, 4096])
    uT = kb.din('uT', [16, 4096])
    vT = kb.din('vT', [D, 16])
    fbT = kb.din('fbT', [16, 16])
    gatev = kb.din('gatev', [1, 1])
    wqkv4 = kb.din('wqkv4', [D, 3 * D])
    bqkv4 = kb.din('bqkv4', [128, 24])
    fT2 = kb.din('fT2', [128, TPC])

    xo = kb.dout('xo', [TPC, D])
    xn2o = kb.dout('xn2o', [TPC, D], BF16)
    q4t = kb.dout('q4t', [D, TPC], BF16)
    k4t = kb.dout('k4t', [D, TPC], BF16)
    v4t = kb.dout('v4t', [D, TPC], BF16)
    sqq4 = kb.dout('sqq4', [HEADS, TPC], F32)
    sqk4 = kb.dout('sqk4', [HEADS, TPC], F32)

    pool = kb.sb('pool', [128, 73728], BF16)
    stage = kb.sb('stage', [128, 1024], F32)
    xs = [kb.sb(f'x{i}', [128, D], F32) for i in range(4)]
    ztl = xs[0]
    ttl = xs[1]
    xn = [kb.sb(f'xn{i}', [128, D], BF16) for i in range(4)]
    lnt = LNTmp(kb, 'ln')
    xnT = kb.sb('xnT', [128, 8 * 512], BF16)
    otb = xnT
    ysb = kb.sb('ysb', [128, 512], F32)
    bosb = kb.sb('bosb', [128, 8], F32)
    bsb4 = kb.sb('bsb4', [128, 24], F32)
    wsc = kb.sb('wsc', [128, 1], F32)
    wrs = kb.sb('wrs', [128, 1], F32)
    wb = kb.sb('wb', [128, 1024], BF16)
    s4 = kb.sb('s4', [128, 4], F32)
    vsb = kb.sb('vsb', [128, 8 * 16], BF16)
    ub1 = kb.sb('ub1', [16, 128], BF16)
    fsb = kb.sb('fsb', [16, 16], BF16)
    gsb = kb.sb('gsb', [1, 1], F32)
    gcol16 = kb.sb('gcol16', [16, 1], F32)
    t1b = kb.sb('t1b', [16, 128], BF16)
    t2b = kb.sb('t2b', [16, 128], BF16)
    rf = kb.sb('rf', [128, 128], F32)
    rr = kb.sb('rr', [128, 32 * 128], BF16)
    ysb2 = kb.sb('ysb2', [128, 128], F32)
    ftile = kb.sb('ftile', [128, 512], F32)
    rf2 = kb.sb('rf2', [128, 512], F32)
    qkvb = kb.sb('qkvb', [128, 512], BF16)
    sqb = kb.sb('sqb', [128, 512], F32)

    pmm = [kb.ps(f'pmm{i}', [128, 512], F32) for i in range(2)]
    pt1 = kb.ps('pt1', [16, 128], F32)
    pt2 = kb.ps('pt2', [16, 128], F32)
    pg = kb.ps('pg', [16, 1], F32)
    ptr_b = kb.ps('ptr_b', [128, 128], BF16)
    ptr_f = kb.ps('ptr_f', [128, 128], F32)
    pst = kb.ps('pst', [2, 512], F32)

    load_wpool(kb, pool, wo3T, D, D, stage, 'wo3', pool_off=65536)
    kb.dma('sp', bosb[:], bo3)
    kb.dma('sp', bsb4[:], bqkv4)
    for kc in range(8):
        kb.dma('sp', stage[:, 0:16], vT[kc * 128:(kc + 1) * 128, :])
        kb.act(vsb[:, kc * 16:(kc + 1) * 16], stage[:, 0:16], AF.Identity)
    kb.dma('sp', stage[0:16, 0:16], fbT)
    kb.act(fsb[:], stage[0:16, 0:16], AF.Identity)
    kb.dma('sp', gsb[:], gatev)
    kb.mm(pg[:], onesrowf[0:1, 0:16], gsb[:], start=True, stop=True)
    kb.dcopy(gcol16[:], pg[:])

    for r in range(32):
        kb.dma('sp', stage[:], w_fc[r * 128:(r + 1) * 128, :])
        kb.red(wsc[:], stage[:], ALU.add, absval=True)
        kb.ts(wsc[:], wsc[:], 1.0 / D, ALU.mult)
        kb.tss(wsc[:], wsc[:], 1e-5, ALU.max)
        kb.recip(wrs[:], wsc[:])
        kb.act(ztl[:], stage[:], AF.Identity, scale=wrs[:])
        kb.tss(ttl[:], ztl[:], 0.5, ALU.is_gt)
        kb.tss(ztl[:], ztl[:], -0.5, ALU.is_lt)
        kb.tt(ttl[:], ttl[:], ztl[:], ALU.subtract)
        kb.act(wb[:], ttl[:], AF.Identity, scale=wsc[:])
        for kc in range(8):
            kb.transp(ptr_b[:], wb[:, kc * 128:(kc + 1) * 128], identb[:])
            kb.dcopy(pool[:, kc * 4096 + r * 128: kc * 4096 + r * 128 + 128], ptr_b[:])
    for r in range(8):
        for cc in range(4):
            kb.dma('sp', stage[:], w_proj[r * 128:(r + 1) * 128, cc * 1024:(cc + 1) * 1024])
            kb.red(s4[:, cc:cc + 1], stage[:], ALU.add, absval=True)
        kb.red(wsc[:], s4[:], ALU.add)
        kb.ts(wsc[:], wsc[:], 1.0 / 4096.0, ALU.mult)
        kb.tss(wsc[:], wsc[:], 1e-5, ALU.max)
        kb.recip(wrs[:], wsc[:])
        for cc in range(4):
            kb.dma('sp', stage[:], w_proj[r * 128:(r + 1) * 128, cc * 1024:(cc + 1) * 1024])
            kb.act(ztl[:], stage[:], AF.Identity, scale=wrs[:])
            kb.tss(ttl[:], ztl[:], 0.5, ALU.is_gt)
            kb.tss(ztl[:], ztl[:], -0.5, ALU.is_lt)
            kb.tt(ttl[:], ttl[:], ztl[:], ALU.subtract)
            kb.act(wb[:], ttl[:], AF.Identity, scale=wsc[:])
            for j in range(8):
                kb.transp(ptr_b[:], wb[:, j * 128:(j + 1) * 128], identb[:])
                kc = cc * 8 + j
                kb.dcopy(pool[:, 32768 + kc * 1024 + r * 128: 32768 + kc * 1024 + r * 128 + 128],
                         ptr_b[:])

    for st in range(NST):
        for tt in range(4):
            ti = st * 4 + tt
            kb.dma('sp', xs[tt][:], x[ti * 128:(ti + 1) * 128, :])
        for kc in range(8):
            kb.dma('sp', otb[:, kc * 512:(kc + 1) * 512],
                   ot[kc * 128:(kc + 1) * 128, st * 512:(st + 1) * 512])
        for oc in range(8):
            p = pmm[oc % 2]
            for kc in range(8):
                kb.mm(p[:], pool[:, 65536 + kc * D + oc * 128: 65536 + kc * D + oc * 128 + 128],
                      otb[:, kc * 512:(kc + 1) * 512], start=(kc == 0), stop=(kc == 7))
            kb.act(ysb[:], p[:], AF.Identity, bias=bosb[:, oc:oc + 1])
            for tt in range(4):
                kb.transp(ptr_f[:], ysb[:, tt * 128:(tt + 1) * 128], identf[:])
                kb.tt(xs[tt][:, oc * 128:(oc + 1) * 128], xs[tt][:, oc * 128:(oc + 1) * 128],
                      ptr_f[:], ALU.add)
        _ln_xnt(kb, st, xs, xn, lnt, xnT, ptr_b, identb)
        for qr in range(4):
            def xsl(kc):
                return xnT[:, kc * 512 + qr * 128: kc * 512 + qr * 128 + 128]
            for kc in range(8):
                kb.mm(pt1[:], vsb[:, kc * 16:(kc + 1) * 16], xsl(kc),
                      start=(kc == 0), stop=(kc == 7))
            kb.dcopy(t1b[:], pt1[:])
            kb.mm(pt2[:], fsb[:], t1b[:], start=True, stop=True)
            kb.act(t2b[:], pt2[:], AF.Identity, scale=gcol16[:])
            for oc in range(32):
                p = pmm[oc % 2]
                kb.dma('sp', stage[0:16, 0:128], uT[:, oc * 128:(oc + 1) * 128])
                kb.act(ub1[:], stage[0:16, 0:128], AF.Identity)
                for kc in range(8):
                    kb.mm(p[:, 0:128], wblk(pool, 4096, kc, oc), xsl(kc),
                          start=(kc == 0), stop=False)
                kb.mm(p[:, 0:128], ub1[:], t2b[:], start=False, stop=True)
                kb.act(rf[:], p[:, 0:128], AF.Relu)
                kb.act(rr[:, oc * 128:(oc + 1) * 128], rf[:], AF.Square)
            for oc in range(8):
                p = pmm[oc % 2]
                for kc in range(32):
                    kb.mm(p[:, 0:128], pool[:, 32768 + kc * 1024 + oc * 128: 32768 + kc * 1024 + oc * 128 + 128],
                          rr[:, kc * 128:(kc + 1) * 128], start=(kc == 0), stop=(kc == 31))
                kb.act(ysb2[:], p[:, 0:128], AF.Identity)
                kb.transp(ptr_f[:], ysb2[:], identf[:])
                kb.tt(xs[qr][:, oc * 128:(oc + 1) * 128], xs[qr][:, oc * 128:(oc + 1) * 128],
                      ptr_f[:], ALU.add)
        for tt in range(4):
            ti = st * 4 + tt
            kb.dma('pool', xo[ti * 128:(ti + 1) * 128, :], xs[tt][:], outdma=True)
        for tt in range(4):
            ti = st * 4 + tt
            ln_tile(kb, xs[tt][:], xn[tt][:], lnt)
            kb.dma('pool', xn2o[ti * 128:(ti + 1) * 128, :], xn[tt][:], outdma=True)

    load_wpool(kb, pool, wqkv4, D, 3 * D, stage, 'wqkv4')
    for st in range(NST):
        kb.dma('sp', ftile[:], fT2[:, st * 512:(st + 1) * 512])
        for tt in range(4):
            ti = st * 4 + tt
            kb.dma('sp', xn[tt][:], xn2o[ti * 128:(ti + 1) * 128, :])
            for fc in range(8):
                kb.transp(ptr_b[:], xn[tt][:, fc * 128:(fc + 1) * 128], identb[:])
                kb.dcopy(xnT[:, fc * 512 + tt * 128: fc * 512 + tt * 128 + 128], ptr_b[:])
        for oc in range(24):
            p = pmm[oc % 2]
            for kc in range(8):
                kb.mm(p[:], wblk(pool, 3 * D, kc, oc), xnT[:, kc * 512:(kc + 1) * 512],
                      start=(kc == 0), stop=(kc == 7))
            dst = [q4t, k4t, v4t][oc // 8]
            if oc < 16:
                kb.act(rf2[:], p[:], AF.Identity, bias=bsb4[:, oc:oc + 1])
                kb.tt(rf2[:], rf2[:], ftile[:], ALU.mult)
                kb.dcopy(qkvb[:], rf2[:])
                kb.act(sqb[:], rf2[:], AF.Square)
                kb.mm(pst[:], ones2[:], sqb[:], start=True, stop=True)
                kb.dcopy(ysb[0:2, :], pst[:])
                dsts = sqq4 if oc < 8 else sqk4
                kb.dma('act', dsts[(oc % 8) * 2:(oc % 8) * 2 + 2, st * 512:(st + 1) * 512],
                       ysb[0:2, :], outdma=True)
            else:
                kb.act(qkvb[:], p[:], AF.Identity, bias=bsb4[:, oc:oc + 1])
            kb.dma('pool', dst[(oc % 8) * 128:(oc % 8) * 128 + 128, st * 512:(st + 1) * 512],
                   qkvb[:], outdma=True)
    return kb


def build_E4():
    kb = KB("E4")
    hc = host_consts()
    identb = kb.constsb('identb', hc['ident_b'])
    identf = kb.constsb('identf', hc['ident_f'])
    onesrowf = kb.constsb('onesrowf', hc['ones_row_f'])
    onescolf = kb.constsb('onescolf', np.ones((128, 1), dtype=np.float32))
    iota16 = kb.constsb('iota16', np.broadcast_to(np.arange(16, dtype=np.float32), (128, 16)).copy())
    iotad16 = kb.constsb('iotad16', np.broadcast_to(15.0 - np.arange(16, dtype=np.float32), (128, 16)).copy())
    sutb = kb.constsb('sutb', (np.arange(128)[:, None] < np.arange(128)[None, :]).astype(np.float32).astype(BF))

    x = kb.din('x', [TPC, D])
    ot = kb.din('ot', [D, TPC], BF16)
    wp4T = kb.din('wp4T', [D, D])
    bp4 = kb.din('bp4', [128, 8])
    wg = kb.din('wg', [D, 16])
    bg4 = kb.din('bg4', [1, 16])

    xo = kb.dout('xo', [TPC, D])
    xn7o = kb.dout('xn7o', [TPC, D], BF16)
    idxo = kb.dout('idxo', [128, 16], F32)
    poso = kb.dout('poso', [128, 16], F32)
    gateo = kb.dout('gateo', [128, 16], F32)
    cnto = kb.dout('cnto', [1, 16], F32)

    pool = kb.sb('pool', [128, 8192], BF16)
    stage = kb.sb('stage', [128, 1024], F32)
    xs = [kb.sb(f'x{i}', [128, D], F32) for i in range(4)]
    xn = [kb.sb(f'xn{i}', [128, D], BF16) for i in range(4)]
    lnt = LNTmp(kb, 'ln')
    xnT = kb.sb('xnT', [128, 8 * 512], BF16)
    otb = kb.sb('otb', [128, 8 * 512], BF16)
    ysb = kb.sb('ysb', [128, 512], F32)
    bosb = kb.sb('bosb', [128, 8], F32)
    gwsb = kb.sb('gwsb', [128, 8 * 16], BF16)
    gbb = kb.sb('gbb', [128, 16], F32)
    logit = kb.sb('logit', [128, 16], F32)
    mrow = kb.sb('mrow', [128, 1], F32)
    nmrow = kb.sb('nmrow', [128, 1], F32)
    zrow = kb.sb('zrow', [128, 1], F32)
    zre = kb.sb('zre', [128, 1], F32)
    prob = kb.sb('prob', [128, 16], F32)
    gsel = kb.sb('gsel', [128, 16], F32)
    psel = kb.sb('psel', [128, 16], F32)
    isel = kb.sb('isel', [128, 16], F32)
    eq1 = kb.sb('eq1', [128, 16], F32)
    tmp16 = kb.sb('tmp16', [128, 16], F32)
    md = kb.sb('md', [128, 1], F32)
    idxc = kb.sb('idxc', [128, 1], F32)
    ohsel = kb.sb('ohsel', [128, 16], BF16)
    ohself = kb.sb('ohself', [128, 16], F32)
    runrow = kb.sb('runrow', [1, 16], F32)
    r1tmp = kb.sb('r1tmp', [1, 16], F32)
    postile = kb.sb('postile', [128, 16], F32)
    p1 = kb.sb('p1', [128, 1], F32)

    pmm = [kb.ps(f'pmm{i}', [128, 512], F32) for i in range(2)]
    ptr_b = kb.ps('ptr_b', [128, 128], BF16)
    ptr_f = kb.ps('ptr_f', [128, 128], F32)
    ppos = kb.ps('ppos', [128, 16], F32)
    pbc = kb.ps('pbc', [128, 16], F32)
    pcnt = kb.ps('pcnt', [1, 16], F32)

    load_wpool(kb, pool, wp4T, D, D, stage, 'wp4')
    kb.dma('sp', bosb[:], bp4)
    for kc in range(8):
        kb.dma('sp', stage[:, 0:16], wg[kc * 128:(kc + 1) * 128, :])
        kb.act(gwsb[:, kc * 16:(kc + 1) * 16], stage[:, 0:16], AF.Identity)
    kb.dma('sp', stage[0:1, 0:16], bg4)
    kb.mm(pbc[:], onesrowf[:], stage[0:1, 0:16], start=True, stop=True)
    kb.dcopy(gbb[:], pbc[:])
    kb.memset(runrow[:], 0.0)

    for st in range(NST):
        for tt in range(4):
            ti = st * 4 + tt
            kb.dma('sp', xs[tt][:], x[ti * 128:(ti + 1) * 128, :])
        for kc in range(8):
            kb.dma('sp', otb[:, kc * 512:(kc + 1) * 512],
                   ot[kc * 128:(kc + 1) * 128, st * 512:(st + 1) * 512])
        _epilogue_wo(kb, st, otb, pool, D, bosb, xs, ysb, pmm, ptr_f, identf, x)
        for tt in range(4):
            ti = st * 4 + tt
            kb.dma('pool', xo[ti * 128:(ti + 1) * 128, :], xs[tt][:], outdma=True)
        _ln_xnt(kb, st, xs, xn, lnt, xnT, ptr_b, identb)
        for tt in range(4):
            ti = st * 4 + tt
            kb.dma('pool', xn7o[ti * 128:(ti + 1) * 128, :], xn[tt][:], outdma=True)
        for tt in range(4):
            ti = st * 4 + tt
            p = pmm[0]
            for kc in range(8):
                kb.mm(p[:, 0:16], xnT[:, kc * 512 + tt * 128: kc * 512 + tt * 128 + 128],
                      gwsb[:, kc * 16:(kc + 1) * 16], start=(kc == 0), stop=(kc == 7))
            kb.dcopy(logit[:], p[:, 0:16])
            kb.tt(logit[:], logit[:], gbb[:], ALU.add)
            kb.red(mrow[:], logit[:], ALU.max)
            kb.ts(nmrow[:], mrow[:], -1.0, ALU.mult)
            kb.act(prob[:], logit[:], AF.Exp, bias=nmrow[:], accum_out=zrow[:])
            kb.recip(zre[:], zrow[:])
            kb.act(prob[:], prob[:], AF.Identity, scale=zre[:])
            kb.red(mrow[:], prob[:], ALU.max)
            kb.ts(eq1[:], prob[:], mrow[:], ALU.is_equal)
            kb.tt(tmp16[:], eq1[:], iotad16[:], ALU.mult)
            kb.red(md[:], tmp16[:], ALU.max)
            kb.ts(idxc[:], md[:], -1.0, ALU.mult, 15.0, ALU.add)
            kb.ts(ohself[:], iota16[:], idxc[:], ALU.is_equal)
            kb.dcopy(ohsel[:], ohself[:])
            kb.mm(ppos[:], sutb[:], ohsel[:], start=True, stop=True)
            kb.mm(pbc[:], onesrowf[:], runrow[:], start=True, stop=True)
            kb.dcopy(postile[:], ppos[:])
            kb.tt(postile[:], postile[:], pbc[:], ALU.add)
            kb.tt(tmp16[:], postile[:], ohself[:], ALU.mult)
            kb.red(p1[:], tmp16[:], ALU.add)
            kb.dcopy(gsel[:, ti:ti + 1], mrow[:])
            kb.dcopy(isel[:, ti:ti + 1], idxc[:])
            kb.dcopy(psel[:, ti:ti + 1], p1[:])
            kb.mm(pcnt[:], onescolf[:], ohself[:], start=True, stop=True)
            kb.tt(r1tmp[:], runrow[:], pcnt[:], ALU.add)
            kb.dcopy(runrow[:], r1tmp[:])
    kb.dma('pool', idxo, isel[:], outdma=True)
    kb.dma('pool', poso, psel[:], outdma=True)
    kb.dma('pool', gateo, gsel[:], outdma=True)
    kb.dma('pool', cnto, runrow[:], outdma=True)
    return kb


def build_M():
    kb = KB("M")
    hc = host_consts()
    identb = kb.constsb('identb', hc['ident_b'])
    identf = kb.constsb('identf', hc['ident_f'])
    onesrowf = kb.constsb('onesrowf', hc['ones_row_f'])
    iota16 = kb.constsb('iota16', np.broadcast_to(np.arange(16, dtype=np.float32), (128, 16)).copy())
    iota1024 = kb.constsb('iota1024', np.broadcast_to(np.arange(1024, dtype=np.float32), (128, 1024)).copy())
    iota128c = kb.constsb('iota128c', np.arange(128, dtype=np.float32).reshape(128, 1))
    sut8 = kb.constsb('sut8', (np.arange(8)[:, None] < np.arange(8)[None, :]).astype(np.float32))

    xn7 = kb.din('xn7', [N, D], BF16)
    idxa_d = kb.din('idxa', [128, 128], F32)
    posa_d = kb.din('posa', [128, 128], F32)
    gatea_d = kb.din('gatea', [128, 128], F32)
    counts8 = kb.din('counts8', [8, 16], F32)
    ecol = kb.din('ecol', [128, 2], F32)
    w1p = kb.din('w1p', [2, D, 4096], F32)
    w2p = kb.din('w2p', [2, 4096, D], F32)
    b1p = kb.din('b1p', [2, 128, 32], F32)
    b2p = kb.din('b2p', [2, 128, 8], F32)

    y2 = kb.dout('y2', [2048, D], BF16)
    desto = kb.dout('desto', [128, 128], F32)
    gko = kb.dout('gko', [128, 128], F32)

    pool = kb.sb('pool', [128, 32768], BF16)
    hb = kb.sb('hb', [128, 32 * 1024], BF16)
    bufT = kb.sb('bufT', [128, 8 * 1024], BF16)
    yfm = kb.sb('yfm', [128, 8 * 1024], BF16)
    stage = kb.sb('stage', [128, 1024], F32)
    idxa = kb.sb('idxs', [128, 128], F32)
    posa = kb.sb('poss', [128, 128], F32)
    gatea = kb.sb('gates', [128, 128], F32)
    posg = kb.sb('posg', [128, 128], F32)
    keep = kb.sb('keep', [128, 128], F32)
    cnts = kb.sb('cnts', [8, 16], F32)
    offsT = kb.sb('offsT', [16, 8], F32)
    orow = kb.sb('orow', [1, 16], F32)
    obc = kb.sb('obc', [128, 16], F32)
    ohsel = kb.sb('ohsel', [128, 16], F32)
    t1 = kb.sb('t1', [128, 1], F32)
    t2 = kb.sb('t2', [128, 1], F32)
    t3 = kb.sb('t3', [128, 1], F32)
    peq = kb.sb('peq', [128, 1024], F32)
    wv = kb.sb('wv', [128, 1], F32)
    tokof = kb.sb('tokof', [128, 8], F32)
    toki = kb.sb('toki', [128, 8], I32)
    buf1 = kb.sb('buf1', [128, 1024], BF16)
    b1sb = kb.sb('b1sb', [128, 32], F32)
    b2sb = kb.sb('b2sb', [128, 8], F32)
    ysb = kb.sb('ysb', [128, 512], F32)
    ybf = kb.sb('ybf', [128, 512], BF16)
    destsb = kb.sb('destsb', [128, 128], F32)
    gksb = kb.sb('gksb', [128, 128], F32)
    esb = kb.sb('esb', [128, 2], F32)

    pmm = [kb.ps(f'pmm{i}', [128, 512], F32) for i in range(2)]
    ptok = kb.ps('ptok', [128, 8], F32)
    poffs = kb.ps('poffs', [16, 8], F32)
    pr1 = kb.ps('pr1', [1, 16], F32)
    pbc16 = kb.ps('pbc16', [128, 16], F32)
    ptr_b = kb.ps('ptr_b', [128, 128], BF16)

    kb.dma('sp', idxa[:], idxa_d)
    kb.dma('sp', posa[:], posa_d)
    kb.dma('sp', gatea[:], gatea_d)
    kb.dma('sp', cnts[:], counts8)
    kb.dma('sp', esb[:], ecol)
    kb.mm(poffs[:], cnts[:], sut8[0:8, 0:8], start=True, stop=True)
    kb.dcopy(offsT[:], poffs[:])

    for g in range(8):
        kb.transp(pr1[:], offsT[:, g:g + 1], identf[:])
        kb.dcopy(orow[:], pr1[:])
        kb.mm(pbc16[:], onesrowf[:], orow[:], start=True, stop=True)
        kb.dcopy(obc[:], pbc16[:])
        for j in range(g * 16, g * 16 + 16):
            kb.ts(ohsel[:], iota16[:], idxa[:, j:j + 1], ALU.is_equal)
            kb.tt(ohsel[:], ohsel[:], obc[:], ALU.mult)
            kb.red(t1[:], ohsel[:], ALU.add)
            kb.tt(posg[:, j:j + 1], posa[:, j:j + 1], t1[:], ALU.add)
            kb.tss(keep[:, j:j + 1], posg[:, j:j + 1], 1024.0, ALU.is_lt)
            kb.tss(t2[:], posg[:, j:j + 1], 1023.0, ALU.min)
            kb.stt(t3[:], idxa[:, j:j + 1], 1024.0, t2[:], ALU.mult, ALU.add)
            kb.tt(t2[:], t3[:], keep[:, j:j + 1], ALU.mult)
            kb.ts(t3[:], keep[:, j:j + 1], -1.0, ALU.mult, 1.0, ALU.add)
            kb.stt(destsb[:, j:j + 1], t3[:], 1.0e9, t2[:], ALU.mult, ALU.add)
            kb.tt(gksb[:, j:j + 1], keep[:, j:j + 1], gatea[:, j:j + 1], ALU.mult)
    kb.dma('pool', desto, destsb[:], outdma=True)
    kb.dma('pool', gko, gksb[:], outdma=True)

    for el in range(2):
        for j in range(128):
            kb.ts(t1[:], idxa[:, j:j + 1], esb[:, el:el + 1], ALU.is_equal)
            kb.tt(t1[:], t1[:], keep[:, j:j + 1], ALU.mult)
            kb.tss(t2[:], iota128c[:], 128.0 * j, ALU.add)
            kb.tt(wv[:], t2[:], t1[:], ALU.mult)
            kb.ts(peq[:], iota1024[:], posg[:, j:j + 1], ALU.is_equal)
            kb.ts(peq[:], peq[:], t1[:], ALU.mult)
            for pc in range(8):
                kb.mm(ptok[:, pc:pc + 1], peq[:, pc * 128:(pc + 1) * 128], wv[:],
                      start=(j == 0), stop=(j == 127))
        kb.dcopy(tokof[:], ptok[:])
        kb.dcopy(toki[:], tokof[:])
        for pc in range(8):
            kb.idma_gather(buf1[:], xn7, toki[:, pc:pc + 1], N - 1)
            for fc in range(8):
                kb.transp(ptr_b[:], buf1[:, fc * 128:(fc + 1) * 128], identb[:])
                kb.dcopy(bufT[:, fc * 1024 + pc * 128: fc * 1024 + pc * 128 + 128], ptr_b[:])
        load_wpool(kb, pool, w1p[el], D, 4096, stage, 'w1')
        kb.dma('sp', b1sb[:], b1p[el])
        kb.dma('sp', b2sb[:], b2p[el])
        for oc in range(32):
            for half in range(2):
                p = pmm[half]
                for kc in range(8):
                    kb.mm(p[:], wblk(pool, 4096, kc, oc),
                          bufT[:, kc * 1024 + half * 512: kc * 1024 + half * 512 + 512],
                          start=(kc == 0), stop=(kc == 7))
                kb.act(hb[:, oc * 1024 + half * 512: oc * 1024 + half * 512 + 512],
                       p[:], AF.Relu, bias=b1sb[:, oc:oc + 1])
        load_wpool(kb, pool, w2p[el], 4096, D, stage, 'w2')
        for oc in range(8):
            for half in range(2):
                p = pmm[half]
                for kc in range(32):
                    kb.mm(p[:], wblk(pool, D, kc, oc),
                          hb[:, kc * 1024 + half * 512: kc * 1024 + half * 512 + 512],
                          start=(kc == 0), stop=(kc == 31))
                kb.act(ybf[:], p[:], AF.Identity, bias=b2sb[:, oc:oc + 1])
                for piece in range(4):
                    pc = half * 4 + piece
                    kb.transp(ptr_b[:], ybf[:, piece * 128:(piece + 1) * 128], identb[:])
                    kb.dcopy(yfm[:, pc * 1024 + oc * 128: pc * 1024 + oc * 128 + 128], ptr_b[:])
        for pc in range(8):
            kb.dma('pool', y2[el * 1024 + pc * 128: el * 1024 + (pc + 1) * 128, :],
                   yfm[:, pc * 1024:(pc + 1) * 1024], outdma=True)
    return kb



def build_F():
    kb = KB("F")
    x = kb.din('x', [TPC, D])
    y2all = kb.din('y2all', [N, D], BF16)
    destf = kb.din('destf', [128, 16], F32)
    gkf = kb.din('gkf', [128, 16], F32)
    out = kb.dout('out', [TPC, D], F32)

    dsb = kb.sb('dsb', [128, 16], F32)
    gsb = kb.sb('gsb', [128, 16], F32)
    di = kb.sb('di', [128, 16], I32)
    ytile = kb.sb('ytile', [128, D], BF16)
    xtile = kb.sb('xtile', [128, D], F32)
    otile = kb.sb('otile', [128, D], F32)

    kb.dma('sp', dsb[:], destf)
    kb.dma('sp', gsb[:], gkf)
    kb.dcopy(di[:], dsb[:])
    for j in range(16):
        kb.memset(ytile[:], 0.0)
        kb.idma_gather(ytile[:], y2all, di[:, j:j + 1], N - 1)
        kb.dma('sp', xtile[:], x[j * 128:(j + 1) * 128, :])
        kb.stt(otile[:], ytile[:], gsb[:, j:j + 1], xtile[:], ALU.mult, ALU.add)
        kb.dma('pool', out[j * 128:(j + 1) * 128, :], otile[:], outdma=True)
    return kb


def _rope_ft2():
    d = 64
    theta = 1.0 / 10000.0 ** (2.0 * np.arange(0, d // 2, dtype=np.float32) / d)
    seq = np.arange(1, S + 1, dtype=np.float32)
    ang = np.einsum('n,d->nd', seq, theta)
    ang = np.concatenate([ang, ang], axis=-1)
    f = (np.cos(ang) + np.sin(ang)).astype(np.float32)
    fT = f.T
    return np.ascontiguousarray(np.vstack([fT, fT]))


def _bcol(b, nch):
    return np.ascontiguousarray(np.asarray(b, np.float32).reshape(nch, 128).T)


DEBUG_TRACE = False
TRACE = {}


def _snap(name, xc):
    if DEBUG_TRACE:
        TRACE[name] = np.concatenate(xc, axis=0).copy()


def kernel(**inputs):
    inp = {k: np.asarray(v) for k, v in inputs.items()}
    x = np.ascontiguousarray(inp['L0_attn_hidden_state'].reshape(N, D).astype(np.float32))
    xc = shard_tokens(x)

    WqT = inp['L0_attn_Wq'].reshape(1024, 1024).T
    WkT = inp['L0_attn_Wk'].reshape(1024, 1024).T
    WvT = inp['L0_attn_Wv'].reshape(1024, 1024).T
    wq = np.ascontiguousarray(np.concatenate([WqT, WkT, WvT], axis=1).astype(np.float32))
    bflat = np.concatenate([inp['L0_attn_bq'].reshape(-1), inp['L0_attn_bk'].reshape(-1),
                            inp['L0_attn_bv'].reshape(-1)]).astype(np.float32)
    res = build_P0().run([dict(x=xc[c], wq=wq, bq=_bcol(bflat, 24)) for c in range(NCORES)])
    qt = [r['qt'] for r in res]; kt = [r['kt'] for r in res]; vt = [r['vt'] for r in res]
    sqq = [r['sqq'] for r in res]; sqk = [r['sqk'] for r in res]

    res = build_attn('A0').run([dict(q_in=assemble_heads(qt, '')[c], k_in=assemble_heads(kt, '')[c],
                                     v_in=assemble_heads(vt, '')[c], sqq_in=assemble_norm(sqq, '')[c],
                                     sqk_in=assemble_norm(sqk, '')[c]) for c in range(NCORES)])
    ot_pc = disassemble_o([r['o_out'] for r in res])

    e0_in = dict(
        woT=np.ascontiguousarray(inp['L0_attn_Wo'].T.astype(np.float32)),
        bo=_bcol(inp['L0_attn_bo'], 8),
        gwT=np.ascontiguousarray(inp['L0_ffn_gate_w'].T.astype(np.float32)),
        gb=np.ascontiguousarray(inp['L0_ffn_gate_b'].reshape(1, 8).astype(np.float32)),
        w1T=np.ascontiguousarray(inp['L0_ffn_w1'].transpose(0, 2, 1).astype(np.float32)),
        w2T=np.ascontiguousarray(inp['L0_ffn_w2'].transpose(0, 2, 1).astype(np.float32)),
        b1h=np.ascontiguousarray(inp['L0_ffn_b1'].reshape(8, 32, 128).transpose(0, 2, 1).astype(np.float32)),
        b2h=np.ascontiguousarray(inp['L0_ffn_b2'].reshape(8, 8, 128).transpose(0, 2, 1).astype(np.float32)),
        wq1=np.ascontiguousarray(np.concatenate([inp['L1_attn_Wq'].T, inp['L1_attn_Wk'].T,
                                                 inp['L1_attn_Wv'].T], axis=1).astype(np.float32)),
        bq1=_bcol(np.concatenate([inp['L1_attn_bq'], inp['L1_attn_bk'], inp['L1_attn_bv']]), 24),
    )
    res = build_E0().run([dict(x=xc[c], ot=ot_pc[c], **e0_in) for c in range(NCORES)])
    xc = [r['xo'] for r in res]
    _snap('x_l0f', xc)
    q1t = [r['q1t'] for r in res]
    kvo = [r['kvo'] for r in res]
    kso = [r['kso'] for r in res]

    e1_in = dict(
        stdp=np.ascontiguousarray(inp['L1_attn_stdp'].reshape(1, 16).astype(np.float32)),
        wo1T=np.ascontiguousarray(inp['L1_attn_Wo'].T.astype(np.float32)),
        bo1=_bcol(inp['L1_attn_bo'], 8),
        wqkv2=np.ascontiguousarray(np.concatenate([inp['L1_ffn_Wq'], inp['L1_ffn_Wk'],
                                                   inp['L1_ffn_Wv']], axis=1).astype(np.float32)),
    )
    res = build_E1().run([dict(x=xc[c], q1t=q1t[c],
                               kvp=np.ascontiguousarray(np.stack([kvo[(c // 4) * 4 + i] for i in range(4)])),
                               ksp=np.ascontiguousarray(np.stack([kso[(c // 4) * 4 + i] for i in range(4)])),
                               **e1_in) for c in range(NCORES)])
    xc = [r['xo'] for r in res]
    _snap('x_l1a', xc)
    q2t = [r['q2t'] for r in res]; k2t = [r['k2t'] for r in res]; v2t = [r['v2t'] for r in res]
    sqq2 = [r['sqq2'] for r in res]; sqk2 = [r['sqk2'] for r in res]

    res = build_attn_q('A2').run([dict(q_in=assemble_heads(q2t, '')[c], k_in=assemble_heads(k2t, '')[c],
                                       v_in=assemble_heads(v2t, '')[c], sqq_in=assemble_norm(sqq2, '')[c],
                                       sqk_in=assemble_norm(sqk2, '')[c]) for c in range(NCORES)])
    ot_pc = disassemble_o([r['o_out'] for r in res])

    perm = []
    for sec in range(3):
        for h in range(HEADS):
            perm.extend(range(h * 192 + sec * 64, h * 192 + sec * 64 + 64))
    perm = np.array(perm)
    wqkv3 = np.ascontiguousarray(inp['L2_attn_Wqkv'][perm].T.astype(np.float32))
    bqkv3 = _bcol(inp['L2_attn_bqkv'][perm], 24)
    e2_in = dict(
        wo2=np.ascontiguousarray(inp['L1_ffn_Wo'].astype(np.float32)),
        wff=np.ascontiguousarray(inp['L1_ffn_Wff'].astype(np.float32)),
        wqkv3=wqkv3, bqkv3=bqkv3,
    )
    res = build_E2().run([dict(x=xc[c], ot=ot_pc[c], **e2_in) for c in range(NCORES)])
    xc = [r['xo'] for r in res]
    _snap('x_l1f', xc)
    q3t = [r['q3t'] for r in res]; k3t = [r['k3t'] for r in res]; v3t = [r['v3t'] for r in res]
    sqq3 = [r['sqq3'] for r in res]; sqk3 = [r['sqk3'] for r in res]

    res = build_attn('A3').run([dict(q_in=assemble_heads(q3t, '')[c], k_in=assemble_heads(k3t, '')[c],
                                     v_in=assemble_heads(v3t, '')[c], sqq_in=assemble_norm(sqq3, '')[c],
                                     sqk_in=assemble_norm(sqk3, '')[c]) for c in range(NCORES)])
    ot_pc = disassemble_o([r['o_out'] for r in res])

    ft2 = _rope_ft2()
    e3_in = dict(
        wo3T=np.ascontiguousarray(inp['L2_attn_Wo'].T.astype(np.float32)),
        bo3=_bcol(inp['L2_attn_bo'], 8),
        w_fc=np.ascontiguousarray(inp['L2_ffn_w_fc'].astype(np.float32)),
        w_proj=np.ascontiguousarray(inp['L2_ffn_w_proj'].astype(np.float32)),
        uT=np.ascontiguousarray(inp['L2_ffn_u'].T.astype(np.float32)),
        vT=np.ascontiguousarray(inp['L2_ffn_v'].T.astype(np.float32)),
        fbT=np.ascontiguousarray(inp['L2_ffn_fast_b'].T.astype(np.float32)),
        gatev=np.ascontiguousarray(np.asarray(inp['L2_ffn_gate'], np.float32).reshape(1, 1)),
        wqkv4=np.ascontiguousarray(inp['L3_attn_W_attn'].T.astype(np.float32)),
        bqkv4=_bcol(inp['L3_attn_b_attn'], 24),
    )
    res = build_E3().run([dict(x=xc[c], ot=ot_pc[c],
                               fT2=np.ascontiguousarray(ft2[:, (c % 4) * TPC:(c % 4) * TPC + TPC]),
                               **e3_in) for c in range(NCORES)])
    xc = [r['xo'] for r in res]
    _snap('x_l2f', xc)
    q4t = [r['q4t'] for r in res]; k4t = [r['k4t'] for r in res]; v4t = [r['v4t'] for r in res]
    sqq4 = [r['sqq4'] for r in res]; sqk4 = [r['sqk4'] for r in res]

    res = build_attn('A4', causal=True).run(
        [dict(q_in=assemble_heads(q4t, '')[c], k_in=assemble_heads(k4t, '')[c],
              v_in=assemble_heads(v4t, '')[c], sqq_in=assemble_norm(sqq4, '')[c],
              sqk_in=assemble_norm(sqk4, '')[c]) for c in range(NCORES)])
    ot_pc = disassemble_o([r['o_out'] for r in res])

    e4_in = dict(
        wp4T=np.ascontiguousarray(inp['L3_attn_W_proj'].T.astype(np.float32)),
        bp4=_bcol(inp['L3_attn_b_proj'], 8),
        wg=np.ascontiguousarray(inp['L3_ffn_Wg'].astype(np.float32)),
        bg4=np.ascontiguousarray(inp['L3_ffn_bg'].reshape(1, 16).astype(np.float32)),
    )
    res = build_E4().run([dict(x=xc[c], ot=ot_pc[c], **e4_in) for c in range(NCORES)])
    xc = [r['xo'] for r in res]
    _snap('x_l3a', xc)
    xn7 = np.ascontiguousarray(np.concatenate([r['xn7o'] for r in res], axis=0))
    idxa = np.ascontiguousarray(np.concatenate([r['idxo'] for r in res], axis=1))
    posa = np.ascontiguousarray(np.concatenate([r['poso'] for r in res], axis=1))
    gatea = np.ascontiguousarray(np.concatenate([r['gateo'] for r in res], axis=1))
    counts8 = np.ascontiguousarray(np.concatenate([r['cnto'] for r in res], axis=0))

    W1 = inp['L3_ffn_W1'].astype(np.float32)
    W2 = inp['L3_ffn_W2'].astype(np.float32)
    b1 = inp['L3_ffn_b1'].astype(np.float32)
    b2 = inp['L3_ffn_b2'].astype(np.float32)
    m_cores = []
    for c in range(NCORES):
        es = [2 * c, 2 * c + 1]
        ecol = np.ascontiguousarray(np.broadcast_to(np.array(es, np.float32), (128, 2)).copy())
        m_cores.append(dict(
            xn7=xn7, idxa=idxa, posa=posa, gatea=gatea, counts8=counts8, ecol=ecol,
            w1p=np.ascontiguousarray(W1[es]),
            w2p=np.ascontiguousarray(W2[es]),
            b1p=np.ascontiguousarray(b1[es].reshape(2, 32, 128).transpose(0, 2, 1)),
            b2p=np.ascontiguousarray(b2[es].reshape(2, 8, 128).transpose(0, 2, 1)),
        ))
    res = build_M().run(m_cores)
    y2all = np.ascontiguousarray(np.concatenate([r['y2'] for r in res], axis=0))
    desta = res[0]['desto']
    gka = res[0]['gko']

    res = build_F().run([dict(x=xc[c], y2all=y2all,
                              destf=np.ascontiguousarray(desta[:, c * 16:(c + 1) * 16]),
                              gkf=np.ascontiguousarray(gka[:, c * 16:(c + 1) * 16]))
                         for c in range(NCORES)])
    out = np.concatenate([r['out'] for r in res], axis=0)
    return np.ascontiguousarray(out.reshape(B, S, D).astype(np.float32))


# version 02 (final)
# speedup vs baseline: 1.3097x; 1.3097x over previous
import numpy as np
import ml_dtypes
import contextlib
import concourse.bass as bass
import concourse.mybir as mybir
from concourse.bass_utils import run_bass_kernel_spmd

F32 = mybir.dt.float32
BF16 = mybir.dt.bfloat16
I32 = mybir.dt.int32
AF = mybir.ActivationFunctionType
ALU = mybir.AluOpType
AX = mybir.AxisListType
BF = ml_dtypes.bfloat16

NCORES = 8
B, S, D = 2, 8192, 1024
N = B * S
TPC = N // NCORES
NT = TPC // 128
NST = TPC // 512
HEADS = 16
HD = 64
UNITS_PER_CORE = (B * HEADS) // NCORES

ENG2BLK = dict(pe='tensor', act='scalar', dve='vector', pool='gpsimd', sp='sync')


class Op:
    __slots__ = ('eng', 'method', 'kw', 'deps', 'dma', 'inc', 'pos', 'outdma')

    def __init__(self, eng, method, kw, deps, dma):
        self.eng, self.method, self.kw, self.deps, self.dma = eng, method, kw, deps, dma
        self.inc = False
        self.pos = 0
        self.outdma = False


def _key(x):
    t = getattr(x, 'tensor', None)
    if t is not None:
        return t.name
    return x.name


class KB:

    def __init__(self, tag=""):
        self.nc = bass.Bass("TRN2", target_bir_lowering=False)
        self.es = contextlib.ExitStack()
        self.ops = []
        self.lastw = {}
        self.readers = {}
        self.consts = {}
        self.out_names = []
        self.sb_bytes = 0
        self.tag = tag

    def din(self, name, shape, dt=F32):
        return self.nc.dram_tensor(name, list(shape), dt, kind="ExternalInput").ap()

    def dout(self, name, shape, dt=F32):
        self.out_names.append(name)
        return self.nc.dram_tensor(name, list(shape), dt, kind="ExternalOutput").ap()

    def const(self, name, arr):
        arr = np.ascontiguousarray(arr)
        dt = {np.dtype(np.float32): F32, np.dtype(BF): BF16,
              np.dtype(np.int32): I32}[arr.dtype]
        self.consts[name] = arr
        return self.nc.dram_tensor(name, list(arr.shape), dt, kind="ExternalInput").ap()

    def constsb(self, name, arr):
        arr = np.ascontiguousarray(arr)
        dt = {np.dtype(np.float32): F32, np.dtype(BF): BF16,
              np.dtype(np.int32): I32}[arr.dtype]
        dr = self.const(name + '_c', arr)
        sbh = self.sb(name, list(arr.shape), dt)
        self.dma('sp', sbh[:], dr)
        return sbh

    def sb(self, name, shape, dt=F32):
        esz = 2 if dt == BF16 else 4
        self.sb_bytes += shape[1] * esz * 128
        return self.es.enter_context(self.nc.sbuf_tensor(name, list(shape), dt))

    def ps(self, name, shape, dt=F32):
        return self.es.enter_context(self.nc.psum_tensor(name, list(shape), dt))

    def _rec(self, eng, method, kw, reads, writes, dma=False):
        idx = len(self.ops)
        deps = set()
        for r in reads:
            if r is None:
                continue
            k = _key(r)
            lw = self.lastw.get(k)
            if lw is not None and not (self.ops[lw].eng == 'pe' and eng == 'pe' and not self.ops[lw].dma and not dma):
                deps.add(lw)
        for w in writes:
            k = _key(w)
            lw = self.lastw.get(k)
            if lw is not None and not (self.ops[lw].eng == 'pe' and eng == 'pe' and not self.ops[lw].dma and not dma):
                deps.add(lw)
            for rd in self.readers.get(k, ()):
                if not (self.ops[rd].eng == 'pe' and eng == 'pe' and not self.ops[rd].dma and not dma):
                    deps.add(rd)
        for w in writes:
            k = _key(w)
            self.lastw[k] = idx
            self.readers[k] = []
        for r in reads:
            if r is None:
                continue
            self.readers.setdefault(_key(r), []).append(idx)
        op = Op(eng, method, kw, deps, dma)
        self.ops.append(op)
        return op

    def mm(self, out, lhsT, rhs, start=True, stop=True):
        self._rec('pe', 'matmul', dict(out=out, lhsT=lhsT, rhs=rhs, start=start, stop=stop),
                  reads=[lhsT, rhs], writes=[out])

    def transp(self, out, in_, ident):
        p = in_.shape[0]
        idap = ident[0:p, 0:p] if ident.shape[0] != p else ident[:, 0:p]
        self._rec('pe', 'transpose', dict(out=out, in_=in_, identity=idap),
                  reads=[in_, ident], writes=[out])

    def act(self, out, in_, func, bias=None, scale=None, accum_out=None):
        kw = dict(out=out, in_=in_, func=func)
        reads = [in_]
        writes = [out]
        if bias is not None:
            kw['bias'] = bias
            if not isinstance(bias, (int, float)):
                reads.append(bias)
        if scale is not None:
            kw['scale'] = scale
            if not isinstance(scale, (int, float)):
                reads.append(scale)
        if accum_out is not None:
            kw['accum_out'] = accum_out
            writes.append(accum_out)
        self._rec('act', 'activation', kw, reads=reads, writes=writes)

    def tt(self, out, a, b, op):
        self._rec('dve', 'tensor_tensor', dict(out=out, in0=a, in1=b, op=op),
                  reads=[a, b], writes=[out])

    def ts(self, out, a, s1, op0, s2=None, op1=None):
        kw = dict(out=out, in0=a, scalar1=s1, scalar2=s2, op0=op0)
        if op1 is not None:
            kw['op1'] = op1
        reads = [a]
        if not isinstance(s1, (int, float)):
            reads.append(s1)
        if s2 is not None and not isinstance(s2, (int, float)):
            reads.append(s2)
        self._rec('dve', 'tensor_scalar', kw, reads=reads, writes=[out])

    def tss(self, out, a, s, op):
        reads = [a]
        if not isinstance(s, (int, float)):
            reads.append(s)
        self._rec('dve', 'tensor_single_scalar', dict(out=out, in_=a, scalar=s, op=op),
                  reads=reads, writes=[out])

    def stt(self, out, in0, s, in1, op0, op1):
        reads = [in0, in1]
        if not isinstance(s, (int, float)):
            reads.append(s)
        self._rec('dve', 'scalar_tensor_tensor', dict(out=out, in0=in0, scalar=s, in1=in1, op0=op0, op1=op1),
                  reads=reads, writes=[out])

    def red(self, out, in_, op, absval=False, negate=False):
        kw = dict(out=out, in_=in_, axis=AX.X, op=op)
        if absval:
            kw['apply_absolute_value'] = True
        if negate:
            kw['negate'] = True
        self._rec('dve', 'tensor_reduce', kw, reads=[in_], writes=[out])

    def recip(self, out, in_):
        self._rec('dve', 'reciprocal', dict(out=out, in_=in_), reads=[in_], writes=[out])

    def dcopy(self, out, in_):
        self._rec('dve', 'tensor_copy', dict(out=out, in_=in_), reads=[in_], writes=[out])

    def memset(self, ap, c):
        self._rec('dve', 'memset', dict(ap=ap, constant=c), reads=[], writes=[ap])

    def dma(self, q, out, in_, outdma=False):
        op = self._rec(q, 'dma_start', dict(out=out, in_=in_), reads=[in_], writes=[out], dma=True)
        op.outdma = outdma
        return op

    def idma_gather(self, out, table, idx, bound, outdma=False):
        op = self._rec('pool', 'indirect_dma_start',
                       dict(out=out, out_offset=None, in_=table,
                            in_offset=bass.IndirectOffsetOnAxis(ap=idx, axis=0),
                            bounds_check=bound, oob_is_err=False),
                       reads=[table, idx], writes=[out], dma=True)
        op.outdma = outdma
        return op

    def build(self):
        ops = self.ops
        for op in ops:
            if op.dma:
                op.inc = True
        for op in ops:
            for d in op.deps:
                ops[d].inc = True
        cnt = {}
        for op in ops:
            k = (op.eng, op.dma)
            if op.inc:
                cnt[k] = cnt.get(k, 0) + (16 if op.dma else 1)
            op.pos = cnt.get(k, 0)
        sem_keys = sorted(set((op.eng, op.dma) for op in ops if op.inc))
        finwaits = {}
        for op in ops:
            if op.outdma:
                k = (op.eng, op.dma)
                finwaits[k] = max(finwaits.get(k, 0), op.pos)
        assert finwaits, "no output DMAs marked"

        sems = {}
        for k in sem_keys:
            sems[k] = self.es.enter_context(self.nc.semaphore(f"s_{k[0]}_{'d' if k[1] else 'c'}"))

        blk = self.es.enter_context(self.nc.Block())
        per_eng = {}
        for i, op in enumerate(ops):
            per_eng.setdefault(op.eng, []).append(op)

        def make_runner(eng, eops, is_fin):
            def runner(e):
                waited = {}
                for op in eops:
                    for d in sorted(op.deps):
                        dep = ops[d]
                        k = (dep.eng, dep.dma)
                        v = dep.pos
                        assert dep.inc
                        if waited.get(k, 0) < v:
                            e.wait_ge(sems[k], v)
                            waited[k] = v
                    inst = getattr(e, op.method)(**op.kw)
                    if op.inc:
                        inst.then_inc(sems[(op.eng, op.dma)], 16 if op.dma else 1)
                if is_fin:
                    for k, v in sorted(finwaits.items()):
                        if waited.get(k, 0) < v or True:
                            e.wait_ge(sems[k], v)
            return runner

        fin_eng = 'pool' if 'pool' in per_eng else sorted(per_eng)[0]
        for eng, eops in per_eng.items():
            getattr(blk, ENG2BLK[eng])(make_runner(eng, eops, eng == fin_eng))
        if fin_eng not in per_eng:
            getattr(blk, ENG2BLK[fin_eng])(make_runner(fin_eng, [], True))
        self.es.close()
        return self.nc

    def run(self, per_core_inputs, trace=False):
        import os as _os
        nc = self.build()
        in_maps = []
        for ci in range(len(per_core_inputs)):
            m = dict(per_core_inputs[ci])
            for cn, arr in self.consts.items():
                m[cn] = arr
            in_maps.append(m)
        if _os.environ.get('KTRACE'):
            trace = True
        res = run_bass_kernel_spmd(nc, in_maps, core_ids=list(range(len(per_core_inputs))), trace=trace)
        if _os.environ.get('KTRACE') and res.exec_time_ns is not None:
            print(f"[KTRACE] {self.tag}: {res.exec_time_ns/1e3:.1f} us", flush=True)
        return res.results



def host_consts():
    c = {}
    c['ident_f'] = np.eye(128, dtype=np.float32)
    c['ident_b'] = np.eye(128, dtype=np.float32).astype(BF)
    c['ones_row_b'] = np.ones((1, 128), dtype=np.float32).astype(BF)
    c['ones_row_f'] = np.ones((1, 128), dtype=np.float32)
    c['ones_col_b'] = np.ones((128, 1), dtype=np.float32).astype(BF)
    o2 = np.zeros((128, 2), dtype=np.float32)
    o2[:64, 0] = 1.0
    o2[64:, 1] = 1.0
    c['ones2_b'] = o2.astype(BF)
    c['ones2_f'] = o2.copy()
    idl = np.zeros((128, 64), dtype=np.float32)
    idl[64:, :] = np.eye(64)
    c['identb64'] = idl.astype(BF)
    return c



class LNTmp:
    def __init__(self, kb, name):
        self.sq = kb.sb(name + "_sq", [128, D], F32)
        self.s1 = kb.sb(name + "_s1", [128, 1], F32)
        self.s2 = kb.sb(name + "_s2", [128, 1], F32)
        self.m = kb.sb(name + "_m", [128, 1], F32)
        self.v = kb.sb(name + "_v", [128, 1], F32)
        self.r = kb.sb(name + "_r", [128, 1], F32)
        self.bb = kb.sb(name + "_bb", [128, 1], F32)


def ln_tile(kb, x_ap, out_ap, t):
    kb.red(t.s1[:], x_ap, ALU.add)
    kb.act(t.sq[:], x_ap, AF.Square)
    kb.red(t.s2[:], t.sq[:], ALU.add)
    kb.ts(t.m[:], t.s1[:], 1.0 / D, ALU.mult)
    kb.ts(t.v[:], t.s2[:], 1.0 / D, ALU.mult)
    kb.stt(t.v[:], t.m[:], t.m[:], t.v[:], ALU.mult, ALU.subtract)
    kb.ts(t.v[:], t.v[:], -1.0, ALU.mult, 1e-5, ALU.add)
    kb.act(t.v[:], t.v[:], AF.Sqrt)
    kb.recip(t.r[:], t.v[:])
    kb.stt(t.bb[:], t.m[:], -1.0, t.r[:], ALU.mult, ALU.mult)
    kb.act(out_ap, x_ap, AF.Identity, bias=t.bb[:], scale=t.r[:])



def load_wpool(kb, pool_ap, dram_ap, rows, cols, stage, tag, pool_off=0, stage2=None,
               queues=('sp', 'sp')):
    kc_n = rows // 128
    stages = [stage, stage2 if stage2 is not None else stage]
    sw = stage.shape[1]
    i = 0
    for kc in range(kc_n):
        for c0 in range(0, cols, sw):
            c1 = min(cols, c0 + sw)
            s = stages[i % 2][:, 0:c1 - c0]
            kb.dma(queues[i % 2], s, dram_ap[kc * 128:(kc + 1) * 128, c0:c1])
            kb.act(pool_ap[:, pool_off + kc * cols + c0: pool_off + kc * cols + c1],
                   s, AF.Identity)
            i += 1


def wblk(pool_ap, cols, kc, oc):
    return pool_ap[:, kc * cols + oc * 128: kc * cols + oc * 128 + 128]



def build_P0():
    kb = KB("P0")
    hc = host_consts()
    identb = kb.constsb('identb', hc['ident_b'])
    ones2 = kb.constsb('ones2', hc['ones2_f'])

    x = kb.din('x', [TPC, D])
    wq = kb.din('wq', [D, 3 * D])
    bq = kb.din('bq', [128, 24])

    qt = kb.dout('qt', [D, TPC], BF16)
    kt = kb.dout('kt', [D, TPC], BF16)
    vt = kb.dout('vt', [D, TPC], BF16)
    sqq = kb.dout('sqq', [HEADS, TPC], F32)
    sqk = kb.dout('sqk', [HEADS, TPC], F32)

    wpool = kb.sb('wpool', [128, 8 * 3 * D], BF16)
    wstage = kb.sb('wstage', [128, 3 * D], F32)
    load_wpool(kb, wpool, wq, D, 3 * D, wstage, 'w')
    bsb = kb.sb('bsb', [128, 24], F32)
    kb.dma('sp', bsb[:], bq)

    xs = [kb.sb(f'x{i}', [128, D], F32) for i in range(4)]
    xn = [kb.sb(f'xn{i}', [128, D], BF16) for i in range(4)]
    lnt = LNTmp(kb, 'ln')
    xnT = kb.sb('xnT', [128, 8 * 512], BF16)
    ptr = kb.ps('ptr', [128, 128], BF16)
    pmm = [kb.ps(f'pmm{i}', [128, 512], F32) for i in range(2)]
    pst = kb.ps('pst', [2, 512], F32)
    obuf = kb.sb('obuf', [128, 512], BF16)
    sqb = kb.sb('sqb', [128, 512], F32)
    stq = kb.sb('stq', [2, 512], F32)

    for st in range(NST):
        for tt in range(4):
            ti = st * 4 + tt
            kb.dma('sp', xs[tt][:], x[ti * 128:(ti + 1) * 128, :])
            ln_tile(kb, xs[tt][:], xn[tt][:], lnt)
            for fc in range(8):
                kb.transp(ptr[:], xn[tt][:, fc * 128:(fc + 1) * 128], identb[:])
                kb.dcopy(xnT[:, fc * 512 + tt * 128: fc * 512 + tt * 128 + 128], ptr[:])
        for oc in range(24):
            p = pmm[oc % 2]
            for kc in range(8):
                kb.mm(p[:], wblk(wpool, 3 * D, kc, oc), xnT[:, kc * 512:(kc + 1) * 512],
                      start=(kc == 0), stop=(kc == 7))
            kb.act(obuf[:], p[:], AF.Identity, bias=bsb[:, oc:oc + 1])
            dst = [qt, kt, vt][oc // 8]
            kb.dma('pool', dst[(oc % 8) * 128:(oc % 8) * 128 + 128, st * 512:(st + 1) * 512],
                   obuf[:], outdma=True)
            if oc < 16:
                kb.act(sqb[:], p[:], AF.Square, bias=bsb[:, oc:oc + 1])
                kb.mm(pst[:], ones2[:], sqb[:], start=True, stop=True)
                kb.dcopy(stq[:], pst[:])
                dsts = sqq if oc < 8 else sqk
                kb.dma('act', dsts[(oc % 8) * 2:(oc % 8) * 2 + 2, st * 512:(st + 1) * 512],
                       stq[:], outdma=True)
    return kb



def build_attn(tag, causal=False, scale=0.125):
    kb = KB(tag)
    hc = host_consts()
    identb = kb.constsb('identb', hc['ident_b'])
    onesrow = kb.constsb('onesrow', hc['ones_row_f'])
    if causal:
        masksA, masksB = [], []
        for d in range(4):
            k_i = np.arange(128)[:, None]
            q_i = np.arange(512)[None, :]
            wedge = (q_i >= k_i + d * 128).astype(np.float32)
            ones = np.ones((128, 512), np.float32)
            zeros = np.zeros((128, 512), np.float32)
            masksA.append(kb.constsb(f'cmA{d}', np.concatenate([wedge, ones], 1).astype(BF)))
            masksB.append(kb.constsb(f'cmB{d}', np.concatenate([zeros, wedge], 1).astype(BF)))

    NU = UNITS_PER_CORE
    q_in = kb.din('q_in', [NU, HD, S], BF16)
    k_in = kb.din('k_in', [NU, HD, S], BF16)
    v_in = kb.din('v_in', [NU, HD, S], BF16)
    sqq_in = kb.din('sqq_in', [NU, 1, S], F32)
    sqk_in = kb.din('sqk_in', [NU, 1, S], F32)
    o_out = kb.dout('o_out', [NU, HD, S], BF16)

    KBLK = S // 128

    qa = [kb.sb(f'qa{i}', [65, S], BF16) for i in range(2)]
    ka = [kb.sb(f'ka{i}', [65, S], BF16) for i in range(2)]
    vts = [kb.sb(f'vts{i}', [64, S], BF16) for i in range(2)]
    vtk = [kb.sb(f'vtk{i}', [128, 65 * KBLK], BF16) for i in range(2)]
    nq = kb.sb('nq', [1, S], BF16)
    nk = kb.sb('nk', [1, S], BF16)
    nchunk = kb.sb('nchunk', [1, 512], F32)
    mkc = kb.sb('mkc', [1, 16], F32)
    mk = kb.sb('mk', [1, 1], F32)
    pS = [kb.ps(f'pS{i}', [128, 1024], F32) for i in range(2)]
    pV = [kb.ps(f'pV{i}', [65, 512], F32) for i in range(2)]
    ptr = kb.ps('ptr', [128, 128], BF16)
    pbc = kb.ps('pbc', [64, 512], F32)
    pexp = kb.sb('pexp', [128, 8 * 1024], BF16)
    oraw = kb.sb('oraw', [65, 512], F32)
    den = kb.sb('den', [1, 512], F32)
    obf = kb.sb('obf', [64, 512], BF16)
    augrow = kb.sb('augrow', [1, S], BF16)

    for u in range(NU):
        ub = u % 2
        kb.dma('sp', qa[ub][0:64, :], q_in[u])
        kb.dma('sp', ka[ub][0:64, :], k_in[u])
        kb.dma('sp', vts[ub][:], v_in[u])
        for cc in range(16):
            kb.dma('sp', nchunk[:], sqq_in[u][:, cc * 512:(cc + 1) * 512])
            kb.act(nq[:, cc * 512:(cc + 1) * 512], nchunk[:], AF.Sqrt)
            kb.dma('sp', nchunk[:], sqk_in[u][:, cc * 512:(cc + 1) * 512])
            kb.act(nk[:, cc * 512:(cc + 1) * 512], nchunk[:], AF.Sqrt)
            kb.red(mkc[:, cc:cc + 1], nk[:, cc * 512:(cc + 1) * 512], ALU.max)
        kb.red(mk[:], mkc[:], ALU.max)
        kb.ts(augrow[:], nq[:], mk[:], ALU.mult, -1.0, ALU.mult)
        kb.dcopy(qa[ub][64:65, :], augrow[:])
        kb.memset(ka[ub][64:65, :], 1.0)
        kb.memset(vtk[ub][:], 1.0)
        for j in range(KBLK):
            kb.transp(ptr[0:128, 0:64], vts[ub][:, j * 128:(j + 1) * 128], identb[:])
            kb.dcopy(vtk[ub][:, j * 65: j * 65 + 64], ptr[0:128, 0:64])

        for qsp in range(0, S // 512, 2):
            jmax = KBLK if not causal else min(KBLK, (qsp + 2) * 4)
            for jb in range(0, jmax, 8):
                je = min(jb + 8, jmax)
                for j in range(jb, je):
                    p = pS[j % 2]
                    kb.mm(p[:, 0:512], ka[ub][:, j * 128:(j + 1) * 128],
                          qa[ub][:, qsp * 512:(qsp + 1) * 512], start=True, stop=True)
                    kb.mm(p[:, 512:1024], ka[ub][:, j * 128:(j + 1) * 128],
                          qa[ub][:, (qsp + 1) * 512:(qsp + 2) * 512], start=True, stop=True)
                    kb.act(pexp[:, (j % 8) * 1024:(j % 8) * 1024 + 1024], p[:], AF.Exp, scale=scale)
                    if causal and j >= qsp * 4:
                        d = j - qsp * 4
                        m = masksA[d][:] if d < 4 else masksB[d - 4][:]
                        kb.tt(pexp[:, (j % 8) * 1024:(j % 8) * 1024 + 1024],
                              pexp[:, (j % 8) * 1024:(j % 8) * 1024 + 1024], m, ALU.mult)
                for j in range(jb, je):
                    jm1 = min(jmax, KBLK if not causal else (qsp + 1) * 4)
                    if j < jm1:
                        kb.mm(pV[0][:], vtk[ub][:, j * 65:(j + 1) * 65],
                              pexp[:, (j % 8) * 1024:(j % 8) * 1024 + 512],
                              start=(j == 0), stop=(j == jm1 - 1))
                    kb.mm(pV[1][:], vtk[ub][:, j * 65:(j + 1) * 65],
                          pexp[:, (j % 8) * 1024 + 512:(j % 8) * 1024 + 1024],
                          start=(j == 0), stop=(j == jmax - 1))
            for half in range(2):
                qs = qsp + half
                pv = pV[half]
                kb.act(oraw[:], pv[:], AF.Identity)
                kb.recip(den[:], oraw[64:65, :])
                kb.mm(pbc[:], onesrow[0:1, 0:64], den[:], start=True, stop=True)
                kb.tt(obf[:], oraw[0:64, :], pbc[:], ALU.mult)
                kb.dma('pool', o_out[u][:, qs * 512:(qs + 1) * 512], obf[:], outdma=True)
    return kb



def build_attn_q(tag, scale=1.0 / 32.0):
    kb = KB(tag)
    hc = host_consts()
    identb = kb.constsb('identb', hc['ident_b'])

    NU = UNITS_PER_CORE
    q_in = kb.din('q_in', [NU, HD, S], BF16)
    k_in = kb.din('k_in', [NU, HD, S], BF16)
    v_in = kb.din('v_in', [NU, HD, S], BF16)
    sqq_in = kb.din('sqq_in', [NU, 1, S], F32)
    sqk_in = kb.din('sqk_in', [NU, 1, S], F32)
    o_out = kb.dout('o_out', [NU, HD, S], BF16)

    KBLK = S // 128
    QST = S // 512

    qa = [kb.sb(f'qa{i}', [65, S], BF16) for i in range(2)]
    ka = [kb.sb(f'ka{i}', [65, S], BF16) for i in range(2)]
    vts = [kb.sb(f'vts{i}', [64, S], BF16) for i in range(2)]
    nq = kb.sb('nq', [1, S], BF16)
    nk = kb.sb('nk', [1, S], BF16)
    nchunk = kb.sb('nchunk', [1, 512], F32)
    mkc = kb.sb('mkc', [1, 16], F32)
    mq = kb.sb('mq', [1, 1], F32)
    augrow = kb.sb('augrow', [1, S], BF16)
    pS = [kb.ps(f'pS{i}', [128, 1024], F32) for i in range(2)]
    pV = [kb.ps(f'pV{i}', [64, 512], F32) for i in range(2)]
    ptr = kb.ps('ptr', [128, 128], BF16)
    pexp = kb.sb('pexp', [128, S], BF16)
    zacc = kb.sb('zacc', [128, 8], F32)
    z = kb.sb('z', [128, 1], F32)
    zr = kb.sb('zr', [128, 1], F32)
    vtk = kb.sb('vtk', [128, 64], BF16)
    vtks = kb.sb('vtks', [128, 64], BF16)
    oacc = kb.sb('oacc', [64, S], F32)
    obf = kb.sb('obf', [64, 512], BF16)

    for u in range(NU):
        ub = u % 2
        kb.dma('sp', qa[ub][0:64, :], q_in[u])
        kb.dma('sp', ka[ub][0:64, :], k_in[u])
        kb.dma('sp', vts[ub][:], v_in[u])
        for cc in range(16):
            kb.dma('sp', nchunk[:], sqk_in[u][:, cc * 512:(cc + 1) * 512])
            kb.act(nk[:, cc * 512:(cc + 1) * 512], nchunk[:], AF.Sqrt)
            kb.dma('sp', nchunk[:], sqq_in[u][:, cc * 512:(cc + 1) * 512])
            kb.act(nq[:, cc * 512:(cc + 1) * 512], nchunk[:], AF.Sqrt)
            kb.red(mkc[:, cc:cc + 1], nq[:, cc * 512:(cc + 1) * 512], ALU.max)
        kb.red(mq[:], mkc[:], ALU.max)
        kb.ts(augrow[:], nk[:], mq[:], ALU.mult, -1.0, ALU.mult)
        kb.dcopy(ka[ub][64:65, :], augrow[:])
        kb.memset(qa[ub][64:65, :], 1.0)
        kb.memset(oacc[:], 0.0)

        for j in range(KBLK):
            for qsp in range(0, QST, 2):
                p = pS[(qsp // 2) % 2]
                kb.mm(p[:, 0:512], ka[ub][:, j * 128:(j + 1) * 128],
                      qa[ub][:, qsp * 512:(qsp + 1) * 512], start=True, stop=True)
                kb.mm(p[:, 512:1024], ka[ub][:, j * 128:(j + 1) * 128],
                      qa[ub][:, (qsp + 1) * 512:(qsp + 2) * 512], start=True, stop=True)
                kb.act(pexp[:, qsp * 512:(qsp + 2) * 512], p[:], AF.Exp, scale=scale,
                       accum_out=zacc[:, qsp // 2:qsp // 2 + 1])
            kb.red(z[:], zacc[:], ALU.add)
            kb.recip(zr[:], z[:])
            kb.transp(ptr[0:128, 0:64], vts[ub][:, j * 128:(j + 1) * 128], identb[:])
            kb.dcopy(vtk[:], ptr[0:128, 0:64])
            kb.act(vtks[:], vtk[:], AF.Identity, scale=zr[:])
            for qs in range(QST):
                pv = pV[qs % 2]
                kb.mm(pv[:], vtks[:], pexp[:, qs * 512:(qs + 1) * 512], start=True, stop=True)
                kb.tt(oacc[:, qs * 512:(qs + 1) * 512], oacc[:, qs * 512:(qs + 1) * 512],
                      pv[:], ALU.add)
        for qs in range(QST):
            kb.dcopy(obf[:], oacc[:, qs * 512:(qs + 1) * 512])
            kb.dma('pool', o_out[u][:, qs * 512:(qs + 1) * 512], obf[:], outdma=True)
    return kb



def shard_tokens(a):
    return [np.ascontiguousarray(a[c * TPC:(c + 1) * TPC]) for c in range(NCORES)]


def assemble_heads(per_core_fmaj, name):
    full = [np.concatenate([per_core_fmaj[b * 4 + i] for i in range(4)], axis=1) for b in range(B)]
    per_core = []
    for c in range(NCORES):
        us = []
        for uu in range(UNITS_PER_CORE):
            unit = c * UNITS_PER_CORE + uu
            b, h = unit // HEADS, unit % HEADS
            us.append(full[b][h * 64:(h + 1) * 64, :])
        per_core.append(np.ascontiguousarray(np.stack(us)))
    return per_core


def assemble_norm(per_core_sq, name):
    full = [np.concatenate([per_core_sq[b * 4 + i] for i in range(4)], axis=1) for b in range(B)]
    per_core = []
    for c in range(NCORES):
        us = []
        for uu in range(UNITS_PER_CORE):
            unit = c * UNITS_PER_CORE + uu
            b, h = unit // HEADS, unit % HEADS
            us.append(full[b][h:h + 1, :])
        per_core.append(np.ascontiguousarray(np.stack(us)))
    return per_core


def disassemble_o(res_list):
    full = {}
    for c in range(NCORES):
        for uu in range(UNITS_PER_CORE):
            unit = c * UNITS_PER_CORE + uu
            b, h = unit // HEADS, unit % HEADS
            full.setdefault(b, {})[h] = res_list[c][uu]
    fm = [np.concatenate([full[b][h] for h in range(HEADS)], axis=0) for b in range(B)]
    out = []
    for c in range(NCORES):
        b = c // 4
        s0 = (c % 4) * TPC
        out.append(np.ascontiguousarray(fm[b][:, s0:s0 + TPC]))
    return out




def _epilogue_wo(kb, st, otb, wpool, wcols, bias_sb, xs, ysb, pmm, ptr_f, identf, xt_in):
    for oc in range(8):
        p = pmm[oc % 2]
        for kc in range(8):
            kb.mm(p[:], wblk(wpool, wcols, kc, oc), otb[:, kc * 512:(kc + 1) * 512],
                  start=(kc == 0), stop=(kc == 7))
        if bias_sb is not None:
            kb.act(ysb[:], p[:], AF.Identity, bias=bias_sb[:, oc:oc + 1])
        else:
            kb.act(ysb[:], p[:], AF.Identity)
        for tt in range(4):
            kb.transp(ptr_f[:], ysb[:, tt * 128:(tt + 1) * 128], identf[:])
            kb.tt(xs[tt][:, oc * 128:(oc + 1) * 128], xs[tt][:, oc * 128:(oc + 1) * 128],
                  ptr_f[:], ALU.add)


def _ln_xnt(kb, st, xs, xn, lnt, xnT, ptr_b, identb):
    for tt in range(4):
        ln_tile(kb, xs[tt][:], xn[tt][:], lnt)
        for fc in range(8):
            kb.transp(ptr_b[:], xn[tt][:, fc * 128:(fc + 1) * 128], identb[:])
            kb.dcopy(xnT[:, fc * 512 + tt * 128: fc * 512 + tt * 128 + 128], ptr_b[:])



def build_E0():
    kb = KB("E0")
    hc = host_consts()
    identb = kb.constsb('identb', hc['ident_b'])
    identf = kb.constsb('identf', hc['ident_f'])
    identb64 = kb.constsb('identb64', hc['identb64'])
    onesrowf = kb.constsb('onesrowf', hc['ones_row_f'])
    iota8 = kb.constsb('iota8', np.broadcast_to(np.arange(8, dtype=np.float32), (128, 8)).copy())

    x = kb.din('x', [TPC, D])
    ot = kb.din('ot', [D, TPC], BF16)
    woT = kb.din('woT', [D, D])
    bo = kb.din('bo', [128, 8])
    gwT = kb.din('gwT', [D, 8])
    gb = kb.din('gb', [1, 8])
    w1T = kb.din('w1T', [8, D, 4096])
    w2T = kb.din('w2T', [8, 4096, D])
    b1h = kb.din('b1h', [8, 128, 32])
    b2h = kb.din('b2h', [8, 128, 8])
    wq1 = kb.din('wq1', [D, 3 * D])
    bq1 = kb.din('bq1', [128, 24])

    xo = kb.dout('xo', [TPC, D])
    x2o = kb.dout('x2o', [TPC, D])
    q1t = kb.dout('q1t', [D, TPC], BF16)
    kvo = kb.dout('kvo', [128, 8 * 64])
    kso = kb.dout('kso', [128, 8])

    pool = kb.sb('pool', [128, 32768], BF16)
    wosb = kb.sb('wosb', [128, 8192], BF16)
    stage = kb.sb('stage', [128, 1024], F32)
    stage2 = kb.sb('stage2', [128, 1024], F32)
    hsb = kb.sb('hsb', [128, 32 * 512], BF16)
    xnT = kb.sb('xnT', [128, 8 * 512], BF16)
    xtl = kb.sb('xtl', [128, 8 * 512], BF16)
    macc = kb.sb('macc', [128, 8 * 512], F32)
    xs = [kb.sb(f'x{i}', [128, D], F32) for i in range(4)]
    xn = [kb.sb(f'xn{i}', [128, D], BF16) for i in range(4)]
    lnt = LNTmp(kb, 'ln')
    ysb = kb.sb('ysb', [128, 512], F32)
    bosb = kb.sb('bosb', [128, 8], F32)
    gbb = kb.sb('gbb', [128, 8], F32)
    gwsb = kb.sb('gwsb', [128, 8 * 8], BF16)
    logit = kb.sb('logit', [128, 8], F32)
    mrow = kb.sb('mrow', [128, 1], F32)
    nmrow = kb.sb('nmrow', [128, 1], F32)
    zrow = kb.sb('zrow', [128, 1], F32)
    zre = kb.sb('zre', [128, 1], F32)
    prob = kb.sb('prob', [128, 8], F32)
    eq1 = kb.sb('eq1', [128, 8], F32)
    tmp8 = kb.sb('tmp8', [128, 8], F32)
    i1 = kb.sb('i1', [128, 1], F32)
    i2 = kb.sb('i2', [128, 1], F32)
    chosen = kb.sb('chosen', [128, 1], F32)
    ohsel = kb.sb('ohsel', [128, 8], F32)
    ssel = kb.sb('ssel', [128, 1], F32)
    scale_sb = kb.sb('scale_sb', [128, 16 * 8], F32)
    mask_sb = kb.sb('mask_sb', [128, 16 * 8], F32)
    scr = kb.sb('scr', [1, 512], F32)
    mcr = kb.sb('mcr', [1, 512], F32)
    b1sb = kb.sb('b1sb', [128, 32], F32)
    b2sb = kb.sb('b2sb', [128, 8], F32)
    ytmp = kb.sb('ytmp', [128, 512], F32)
    bq1sb = kb.sb('bq1sb', [128, 24], F32)
    zb = kb.sb('zb', [128, 512], F32)

    qkvb = kb.sb('qkvb', [128, 512], BF16)
    ksb = kb.sb('ksb', [128, 8 * 512], BF16)
    ktok = kb.sb('ktok', [128, 64], BF16)
    vtok = kb.sb('vtok', [128, 64], BF16)
    kvacc = kb.sb('kvacc', [128, 8 * 64], F32)
    ksacc = kb.sb('ksacc', [128, 8], F32)
    red1 = kb.sb('red1', [128, 1], F32)

    pmm = [kb.ps(f'pmm{i}', [128, 512], F32) for i in range(2)]
    ptr_b = kb.ps('ptr_b', [128, 128], BF16)
    ptr_f = kb.ps('ptr_f', [128, 128], F32)
    pbc1 = kb.ps('pbc1', [128, 512], F32)
    pbc2 = kb.ps('pbc2', [128, 512], F32)
    pkv = kb.ps('pkv', [64, 64], F32)
    pr1 = kb.ps('pr1', [1, 512], F32)

    load_wpool(kb, wosb, woT, D, D, stage, 'wo')
    kb.dma('sp', bosb[:], bo)
    kb.dma('sp', bq1sb[:], bq1)
    for kc in range(8):
        kb.dma('sp', stage[:, 0:8], gwT[kc * 128:(kc + 1) * 128, :])
        kb.act(gwsb[:, kc * 8:(kc + 1) * 8], stage[:, 0:8], AF.Identity)
    kb.dma('sp', stage[0:1, 0:8], gb)
    kb.mm(pbc1[:, 0:8], onesrowf[:], stage[0:1, 0:8], start=True, stop=True)
    kb.dcopy(gbb[:], pbc1[:, 0:8])
    kb.memset(kvacc[:], 0.0)
    kb.memset(ksacc[:], 0.0)

    for st in range(NST):
        for tt in range(4):
            ti = st * 4 + tt
            kb.dma('sp', xs[tt][:], x[ti * 128:(ti + 1) * 128, :])
        for kc in range(8):
            kb.dma('sp', xtl[:, kc * 512:(kc + 1) * 512],
                   ot[kc * 128:(kc + 1) * 128, st * 512:(st + 1) * 512])
        _epilogue_wo(kb, st, xtl, wosb, D, bosb, xs, ysb, pmm, ptr_f, identf, x)
        for tt in range(4):
            ti = st * 4 + tt
            kb.dma('pool', x2o[ti * 128:(ti + 1) * 128, :], xs[tt][:], outdma=True)
        _ln_xnt(kb, st, xs, xn, lnt, xnT, ptr_b, identb)
        for tt in range(4):
            ti = st * 4 + tt
            p = pmm[0]
            for kc in range(8):
                kb.mm(p[:, 0:8], xnT[:, kc * 512 + tt * 128: kc * 512 + tt * 128 + 128],
                      gwsb[:, kc * 8:(kc + 1) * 8], start=(kc == 0), stop=(kc == 7))
            kb.dcopy(logit[:], p[:, 0:8])
            kb.tt(logit[:], logit[:], gbb[:], ALU.add)
            kb.red(mrow[:], logit[:], ALU.max)
            kb.ts(nmrow[:], mrow[:], -1.0, ALU.mult)
            kb.act(prob[:], logit[:], AF.Exp, bias=nmrow[:], accum_out=zrow[:])
            kb.recip(zre[:], zrow[:])
            kb.act(prob[:], prob[:], AF.Identity, scale=zre[:])
            kb.red(mrow[:], prob[:], ALU.max)
            kb.ts(eq1[:], prob[:], mrow[:], ALU.is_equal)
            kb.tt(tmp8[:], eq1[:], iota8[:], ALU.mult)
            kb.red(i1[:], tmp8[:], ALU.max)
            kb.stt(tmp8[:], eq1[:], -1e9, prob[:], ALU.mult, ALU.add)
            kb.red(mrow[:], tmp8[:], ALU.max)
            kb.ts(eq1[:], tmp8[:], mrow[:], ALU.is_equal)
            kb.tt(tmp8[:], eq1[:], iota8[:], ALU.mult)
            kb.red(i2[:], tmp8[:], ALU.max)
            kb.tt(chosen[:], i1[:], i2[:], ALU.max)
            kb.ts(ohsel[:], iota8[:], chosen[:], ALU.is_equal)
            kb.tt(tmp8[:], ohsel[:], prob[:], ALU.mult)
            kb.red(ssel[:], tmp8[:], ALU.add)
            for e in range(8):
                kb.ts(scale_sb[:, e * 16 + ti:e * 16 + ti + 1], ohsel[:, e:e + 1],
                      ssel[:], ALU.mult)
                kb.dcopy(mask_sb[:, e * 16 + ti:e * 16 + ti + 1], ohsel[:, e:e + 1])
        import os as _os
        kb.memset(macc[:], 0.0)
        for e in ([] if _os.environ.get('DBG_SKIP_EXPERTS') else range(int(_os.environ.get('DBG_NEXP', '8')))):
            for tt in range(4):
                ti = st * 4 + tt
                kb.transp(pr1[0:1, tt * 128:(tt + 1) * 128],
                          scale_sb[:, e * 16 + ti:e * 16 + ti + 1], identf[:])
            kb.dcopy(scr[:], pr1[:])
            for tt in range(4):
                ti = st * 4 + tt
                kb.transp(pr1[0:1, tt * 128:(tt + 1) * 128],
                          mask_sb[:, e * 16 + ti:e * 16 + ti + 1], identf[:])
            kb.dcopy(mcr[:], pr1[:])
            kb.mm(pbc1[:], onesrowf[:], scr[:], start=True, stop=True)
            kb.mm(pbc2[:], onesrowf[:], mcr[:], start=True, stop=True)
            for kc in range(8):
                kb.tt(xtl[:, kc * 512:(kc + 1) * 512], xnT[:, kc * 512:(kc + 1) * 512],
                      pbc1[:], ALU.mult)
            load_wpool(kb, pool, w1T[e], D, 4096, stage, 'w1', stage2=stage2)
            kb.dma('sp', b1sb[:], b1h[e])
            kb.dma('sp', b2sb[:], b2h[e])
            for oc in range(32):
                p = pmm[oc % 2]
                for kc in range(8):
                    kb.mm(p[:], wblk(pool, 4096, kc, oc),
                          xtl[:, kc * 512:(kc + 1) * 512], start=(kc == 0), stop=(kc == 7))
                kb.act(hsb[:, oc * 512:(oc + 1) * 512], p[:], AF.Relu, bias=b1sb[:, oc:oc + 1])
            load_wpool(kb, pool, w2T[e], 4096, 1024, stage, 'w2', stage2=stage2)
            for oc in range(8):
                p = pmm[oc % 2]
                for kc in range(32):
                    kb.mm(p[:], wblk(pool, 1024, kc, oc),
                          hsb[:, kc * 512:(kc + 1) * 512], start=(kc == 0), stop=(kc == 31))
                kb.act(ytmp[:], p[:], AF.Identity, bias=b2sb[:, oc:oc + 1])
                kb.tt(ytmp[:], ytmp[:], pbc2[:], ALU.mult)
                kb.tt(macc[:, oc * 512:(oc + 1) * 512], macc[:, oc * 512:(oc + 1) * 512],
                      ytmp[:], ALU.add)
        for oc in range(8):
            for tt in range(4):
                kb.transp(ptr_f[:], macc[:, oc * 512 + tt * 128: oc * 512 + tt * 128 + 128],
                          identf[:])
                kb.tt(xs[tt][:, oc * 128:(oc + 1) * 128], xs[tt][:, oc * 128:(oc + 1) * 128],
                      ptr_f[:], ALU.add)
        for tt in range(4):
            ti = st * 4 + tt
            kb.dma('pool', xo[ti * 128:(ti + 1) * 128, :], xs[tt][:], outdma=True)
        _ln_xnt(kb, st, xs, xn, lnt, xnT, ptr_b, identb)
        load_wpool(kb, pool, wq1, D, 3 * D, stage, 'wq1')
        for oc in range(24):
            p = pmm[oc % 2]
            for kc in range(8):
                kb.mm(p[:], wblk(pool, 3 * D, kc, oc), xnT[:, kc * 512:(kc + 1) * 512],
                      start=(kc == 0), stop=(kc == 7))
            kb.act(zb[:], p[:], AF.Identity, bias=bq1sb[:, oc:oc + 1])
            if oc < 16:
                kb.act(ytmp[:], zb[:], AF.Relu)
                kb.tss(ysb[:], zb[:], 0.0, ALU.min)
                kb.act(macc[:, 0:512], ysb[:], AF.Exp)
                kb.tt(qkvb[:], ytmp[:], macc[:, 0:512], ALU.add)
            else:
                kb.dcopy(qkvb[:], zb[:])
            if oc < 8:
                kb.dma('pool', q1t[oc * 128:(oc + 1) * 128, st * 512:(st + 1) * 512],
                       qkvb[:], outdma=True)
            elif oc < 16:
                kb.dcopy(ksb[:, (oc - 8) * 512:(oc - 7) * 512], qkvb[:])
                kb.red(red1[:], qkvb[:], ALU.add)
                kb.tt(ksacc[:, oc - 8:oc - 7], ksacc[:, oc - 8:oc - 7], red1[:], ALU.add)
            else:
                ch = oc - 16
                for hh in range(2):
                    h = ch * 2 + hh
                    pb = hh * 64
                    idh = identb[:] if pb == 0 else identb64[64:128, :]
                    for tt in range(4):
                        kb.transp(ptr_b[0:128, 0:64],
                                  ksb[pb:pb + 64, ch * 512 + tt * 128: ch * 512 + tt * 128 + 128],
                                  idh)
                        kb.dcopy(ktok[:], ptr_b[0:128, 0:64])
                        kb.transp(ptr_b[0:128, 0:64],
                                  qkvb[pb:pb + 64, tt * 128:(tt + 1) * 128],
                                  idh)
                        kb.dcopy(vtok[:], ptr_b[0:128, 0:64])
                        kb.mm(pkv[:], ktok[:], vtok[:], start=True, stop=True)
                        kvs = kvacc[(h % 2) * 64:(h % 2) * 64 + 64, (h // 2) * 64:(h // 2) * 64 + 64]
                        kb.tt(kvs, kvs, pkv[:], ALU.add)
    kb.dma('pool', kvo, kvacc[:], outdma=True)
    kb.dma('pool', kso, ksacc[:], outdma=True)
    return kb


def build_E1():
    kb = KB("E1")
    hc = host_consts()
    identb = kb.constsb('identb', hc['ident_b'])
    identf = kb.constsb('identf', hc['ident_f'])
    onesrowf = kb.constsb('onesrowf', hc['ones_row_f'])
    ones2 = kb.constsb('ones2', hc['ones2_f'])

    x = kb.din('x', [TPC, D])
    q1t = kb.din('q1t', [D, TPC], BF16)
    kvp = kb.din('kvp', [4, 128, 512])
    ksp = kb.din('ksp', [4, 128, 8])
    stdp = kb.din('stdp', [1, 16])
    wo1T = kb.din('wo1T', [D, D])
    bo1 = kb.din('bo1', [128, 8])
    wqkv2 = kb.din('wqkv2', [D, 3 * D])

    xo = kb.dout('xo', [TPC, D])
    q2t = kb.dout('q2t', [D, TPC], BF16)
    k2t = kb.dout('k2t', [D, TPC], BF16)
    v2t = kb.dout('v2t', [D, TPC], BF16)
    sqq2 = kb.dout('sqq2', [HEADS, TPC], F32)
    sqk2 = kb.dout('sqk2', [HEADS, TPC], F32)

    pool = kb.sb('pool', [128, 8192 + 24576], BF16)
    stage = kb.sb('stage', [128, 1024], F32)
    xs = [kb.sb(f'x{i}', [128, D], F32) for i in range(4)]
    xn = [kb.sb(f'xn{i}', [128, D], BF16) for i in range(4)]
    lnt = LNTmp(kb, 'ln')
    xnT = kb.sb('xnT', [128, 8 * 512], BF16)
    qsb = kb.sb('qsb', [128, 8 * 512], BF16)
    o1sb = kb.sb('o1sb', [128, 8 * 512], BF16)
    ysb = kb.sb('ysb', [128, 512], F32)
    bosb = kb.sb('bosb', [128, 8], F32)
    kvt = kb.sb('kvt', [128, 512], F32)
    kvf = kb.sb('kvf', [128, 512], F32)
    kvb = kb.sb('kvb', [128, 512], BF16)
    kst = kb.sb('kst', [128, 8], F32)
    ksf = kb.sb('ksf', [128, 8], F32)
    ksb = kb.sb('ksb', [128, 8], BF16)
    sg = kb.sb('sg', [1, 16], F32)
    drow = kb.sb('drow', [1, 512], F32)
    numsb = kb.sb('numsb', [64, 512], F32)
    qkvb = kb.sb('qkvb', [128, 512], BF16)
    sqb = kb.sb('sqb', [128, 512], F32)
    stq = kb.sb('stq', [2, 512], F32)

    pmm = [kb.ps(f'pmm{i}', [128, 512], F32) for i in range(2)]
    pnum = kb.ps('pnum', [64, 512], F32)
    pden = kb.ps('pden', [1, 512], F32)
    pbc = kb.ps('pbc', [64, 512], F32)
    ptr_b = kb.ps('ptr_b', [128, 128], BF16)
    ptr_f = kb.ps('ptr_f', [128, 128], F32)
    pst = kb.ps('pst', [2, 512], F32)

    kb.dma('sp', kvf[:], kvp[0])
    for i in range(1, 4):
        kb.dma('sp', kvt[:], kvp[i])
        kb.tt(kvf[:], kvf[:], kvt[:], ALU.add)
    kb.dcopy(kvb[:], kvf[:])
    kb.dma('sp', ksf[:], ksp[0])
    for i in range(1, 4):
        kb.dma('sp', kst[:], ksp[i])
        kb.tt(ksf[:], ksf[:], kst[:], ALU.add)
    kb.dcopy(ksb[:], ksf[:])
    kb.dma('sp', sg[:], stdp)
    kb.act(sg[:], sg[:], AF.Sigmoid)
    load_wpool(kb, pool, wo1T, D, D, stage, 'wo1')
    kb.dma('sp', bosb[:], bo1)

    for st in range(NST):
        for tt in range(4):
            ti = st * 4 + tt
            kb.dma('sp', xs[tt][:], x[ti * 128:(ti + 1) * 128, :])
        for kc in range(8):
            kb.dma('sp', qsb[:, kc * 512:(kc + 1) * 512],
                   q1t[kc * 128:(kc + 1) * 128, st * 512:(st + 1) * 512])
        for h in range(HEADS):
            pb = (h % 2) * 64
            ch = h // 2
            qh = qsb[pb:pb + 64, ch * 512:(ch + 1) * 512]
            kvh = kvb[pb:pb + 64, (h // 2) * 64:(h // 2) * 64 + 64]
            ksh = ksb[pb:pb + 64, h // 2:h // 2 + 1]
            kb.mm(pnum[:], kvh, qh, start=True, stop=True)
            kb.mm(pden[:], ksh, qh, start=True, stop=True)
            kb.ts(drow[:], pden[:], 1e-6, ALU.add)
            kb.recip(drow[:], drow[:])
            kb.ts(drow[:], drow[:], sg[0:1, h:h + 1], ALU.mult)
            kb.mm(pbc[:], onesrowf[0:1, 0:64], drow[:], start=True, stop=True)
            kb.act(numsb[:], pnum[:], AF.Identity)
            kb.tt(o1sb[pb:pb + 64, ch * 512:(ch + 1) * 512], numsb[:], pbc[:], ALU.mult)
        _epilogue_wo(kb, st, o1sb, pool, D, bosb, xs, ysb, pmm, ptr_f, identf, x)
        for tt in range(4):
            ti = st * 4 + tt
            kb.dma('pool', xo[ti * 128:(ti + 1) * 128, :], xs[tt][:], outdma=True)
        _ln_xnt(kb, st, xs, xn, lnt, xnT, ptr_b, identb)
        if st == 0:
            load_wpool(kb, pool, wqkv2, D, 3 * D, stage, 'wqkv2', pool_off=8192)
        for oc in range(24):
            p = pmm[oc % 2]
            for kc in range(8):
                kb.mm(p[:], pool[:, 8192 + kc * 3 * D + oc * 128: 8192 + kc * 3 * D + oc * 128 + 128],
                      xnT[:, kc * 512:(kc + 1) * 512], start=(kc == 0), stop=(kc == 7))
            kb.act(qkvb[:], p[:], AF.Identity)
            dst = [q2t, k2t, v2t][oc // 8]
            kb.dma('pool', dst[(oc % 8) * 128:(oc % 8) * 128 + 128, st * 512:(st + 1) * 512],
                   qkvb[:], outdma=True)
            if oc < 16:
                kb.act(sqb[:], p[:], AF.Square)
                kb.mm(pst[:], ones2[:], sqb[:], start=True, stop=True)
                kb.dcopy(stq[:], pst[:])
                dsts = sqq2 if oc < 8 else sqk2
                kb.dma('act', dsts[(oc % 8) * 2:(oc % 8) * 2 + 2, st * 512:(st + 1) * 512],
                       stq[:], outdma=True)
    return kb



def build_E2():
    kb = KB("E2")
    hc = host_consts()
    identb = kb.constsb('identb', hc['ident_b'])
    identf = kb.constsb('identf', hc['ident_f'])
    onesrowf = kb.constsb('onesrowf', hc['ones_row_f'])
    onescolf = kb.constsb('onescolf', np.ones((128, 1), dtype=np.float32))
    ones2 = kb.constsb('ones2', hc['ones2_f'])

    x = kb.din('x', [TPC, D])
    ot = kb.din('ot', [D, TPC], BF16)
    wo2 = kb.din('wo2', [D, D])
    wff = kb.din('wff', [D, D])
    wqkv3 = kb.din('wqkv3', [D, 3 * D])
    bqkv3 = kb.din('bqkv3', [128, 24])

    xo = kb.dout('xo', [TPC, D])
    q3t = kb.dout('q3t', [D, TPC], BF16)
    k3t = kb.dout('k3t', [D, TPC], BF16)
    v3t = kb.dout('v3t', [D, TPC], BF16)
    sqq3 = kb.dout('sqq3', [HEADS, TPC], F32)
    sqk3 = kb.dout('sqk3', [HEADS, TPC], F32)

    pool = kb.sb('pool', [128, 8192 + 8192 + 24576], BF16)
    stage = kb.sb('stage', [128, 1024], F32)
    xs = [kb.sb(f'x{i}', [128, D], F32) for i in range(4)]
    xn = [kb.sb(f'xn{i}', [128, D], BF16) for i in range(4)]
    lnt = LNTmp(kb, 'ln')
    xnT = kb.sb('xnT', [128, 8 * 512], BF16)
    otb = kb.sb('otb', [128, 8 * 512], BF16)
    zsb = kb.sb('zsb', [128, 8 * 512], F32)
    znb = kb.sb('znb', [128, 8 * 512], BF16)
    sqch = kb.sb('sqch', [128, 512], F32)
    nrow = kb.sb('nrow', [1, 512], F32)
    zzf = kb.sb('zzf', [128, 512], F32)
    gf = kb.sb('gf', [128, 512], F32)
    qkvb = kb.sb('qkvb', [128, 512], BF16)
    sqb = kb.sb('sqb', [128, 512], F32)
    stq = kb.sb('stq', [2, 512], F32)
    bsb = kb.sb('bsb', [128, 24], F32)

    pmm = [kb.ps(f'pmm{i}', [128, 512], F32) for i in range(2)]
    pnrm = kb.ps('pnrm', [1, 512], F32)
    pbc = kb.ps('pbc', [128, 512], F32)
    ptr_b = kb.ps('ptr_b', [128, 128], BF16)
    ptr_f = kb.ps('ptr_f', [128, 128], F32)
    pst = kb.ps('pst', [2, 512], F32)

    load_wpool(kb, pool, wo2, D, D, stage, 'wo2')
    load_wpool(kb, pool, wff, D, D, stage, 'wff', pool_off=8192)
    load_wpool(kb, pool, wqkv3, D, 3 * D, stage, 'wqkv3', pool_off=16384)
    kb.dma('sp', bsb[:], bqkv3)

    def l2norm_chunks(src, dstbf):
        for kc in range(8):
            kb.act(sqch[:], src[:, kc * 512:(kc + 1) * 512], AF.Square)
            kb.mm(pnrm[:], onescolf[:], sqch[:], start=(kc == 0), stop=(kc == 7))
        kb.act(nrow[:], pnrm[:], AF.Sqrt)
        kb.tss(nrow[:], nrow[:], 1e-12, ALU.max)
        kb.recip(nrow[:], nrow[:])
        kb.mm(pbc[:], onesrowf[:], nrow[:], start=True, stop=True)

    for st in range(NST):
        for tt in range(4):
            ti = st * 4 + tt
            kb.dma('sp', xs[tt][:], x[ti * 128:(ti + 1) * 128, :])
        for kc in range(8):
            kb.dma('sp', otb[:, kc * 512:(kc + 1) * 512],
                   ot[kc * 128:(kc + 1) * 128, st * 512:(st + 1) * 512])
        for oc in range(8):
            p = pmm[oc % 2]
            for kc in range(8):
                kb.mm(p[:], wblk(pool, D, kc, oc), otb[:, kc * 512:(kc + 1) * 512],
                      start=(kc == 0), stop=(kc == 7))
            kb.act(zsb[:, oc * 512:(oc + 1) * 512], p[:], AF.Identity)
        l2norm_chunks(zsb, True)
        for kc in range(8):
            kb.tt(znb[:, kc * 512:(kc + 1) * 512], zsb[:, kc * 512:(kc + 1) * 512],
                  pbc[:], ALU.mult)
        for oc in range(8):
            p = pmm[oc % 2]
            for kc in range(8):
                kb.mm(p[:], pool[:, 8192 + kc * D + oc * 128: 8192 + kc * D + oc * 128 + 128],
                      znb[:, kc * 512:(kc + 1) * 512], start=(kc == 0), stop=(kc == 7))
            kb.act(zsb[:, oc * 512:(oc + 1) * 512], p[:], AF.Identity)
        l2norm_chunks(zsb, False)
        for kc in range(8):
            kb.tt(zzf[:], zsb[:, kc * 512:(kc + 1) * 512], pbc[:], ALU.mult)
            kb.act(gf[:], zzf[:], AF.Gelu)
            for tt in range(4):
                kb.transp(ptr_f[:], gf[:, tt * 128:(tt + 1) * 128], identf[:])
                kb.tt(xs[tt][:, kc * 128:(kc + 1) * 128], xs[tt][:, kc * 128:(kc + 1) * 128],
                      ptr_f[:], ALU.add)
        for tt in range(4):
            ti = st * 4 + tt
            kb.dma('pool', xo[ti * 128:(ti + 1) * 128, :], xs[tt][:], outdma=True)
        _ln_xnt(kb, st, xs, xn, lnt, xnT, ptr_b, identb)
        for oc in range(24):
            p = pmm[oc % 2]
            for kc in range(8):
                kb.mm(p[:], pool[:, 16384 + kc * 3 * D + oc * 128: 16384 + kc * 3 * D + oc * 128 + 128],
                      xnT[:, kc * 512:(kc + 1) * 512], start=(kc == 0), stop=(kc == 7))
            kb.act(qkvb[:], p[:], AF.Identity, bias=bsb[:, oc:oc + 1])
            dst = [q3t, k3t, v3t][oc // 8]
            kb.dma('pool', dst[(oc % 8) * 128:(oc % 8) * 128 + 128, st * 512:(st + 1) * 512],
                   qkvb[:], outdma=True)
            if oc < 16:
                kb.act(sqb[:], p[:], AF.Square, bias=bsb[:, oc:oc + 1])
                kb.mm(pst[:], ones2[:], sqb[:], start=True, stop=True)
                kb.dcopy(stq[:], pst[:])
                dsts = sqq3 if oc < 8 else sqk3
                kb.dma('act', dsts[(oc % 8) * 2:(oc % 8) * 2 + 2, st * 512:(st + 1) * 512],
                       stq[:], outdma=True)
    return kb


def build_E3():
    kb = KB("E3")
    hc = host_consts()
    identb = kb.constsb('identb', hc['ident_b'])
    identf = kb.constsb('identf', hc['ident_f'])
    onesrowf = kb.constsb('onesrowf', hc['ones_row_f'])
    ones2 = kb.constsb('ones2', hc['ones2_f'])

    x = kb.din('x', [TPC, D])
    ot = kb.din('ot', [D, TPC], BF16)
    wo3T = kb.din('wo3T', [D, D])
    bo3 = kb.din('bo3', [128, 8])
    w_fc = kb.din('w_fc', [4096, D])
    w_proj = kb.din('w_proj', [D, 4096])
    uT = kb.din('uT', [16, 4096])
    vT = kb.din('vT', [D, 16])
    fbT = kb.din('fbT', [16, 16])
    gatev = kb.din('gatev', [1, 1])
    wqkv4 = kb.din('wqkv4', [D, 3 * D])
    bqkv4 = kb.din('bqkv4', [128, 24])
    fT2 = kb.din('fT2', [128, TPC])

    xo = kb.dout('xo', [TPC, D])
    xn2o = kb.dout('xn2o', [TPC, D], BF16)
    q4t = kb.dout('q4t', [D, TPC], BF16)
    k4t = kb.dout('k4t', [D, TPC], BF16)
    v4t = kb.dout('v4t', [D, TPC], BF16)
    sqq4 = kb.dout('sqq4', [HEADS, TPC], F32)
    sqk4 = kb.dout('sqk4', [HEADS, TPC], F32)

    pool = kb.sb('pool', [128, 73728], BF16)
    stage = kb.sb('stage', [128, 1024], F32)
    xs = [kb.sb(f'x{i}', [128, D], F32) for i in range(4)]
    ztl = xs[0]
    ttl = xs[1]
    xn = [kb.sb(f'xn{i}', [128, D], BF16) for i in range(4)]
    lnt = LNTmp(kb, 'ln')
    xnT = kb.sb('xnT', [128, 8 * 512], BF16)
    otb = xnT
    ysb = kb.sb('ysb', [128, 512], F32)
    bosb = kb.sb('bosb', [128, 8], F32)
    bsb4 = kb.sb('bsb4', [128, 24], F32)
    wsc = kb.sb('wsc', [128, 1], F32)
    wrs = kb.sb('wrs', [128, 1], F32)
    wb = kb.sb('wb', [128, 1024], BF16)
    s4 = kb.sb('s4', [128, 4], F32)
    vsb = kb.sb('vsb', [128, 8 * 16], BF16)
    ub1 = kb.sb('ub1', [16, 128], BF16)
    fsb = kb.sb('fsb', [16, 16], BF16)
    gsb = kb.sb('gsb', [1, 1], F32)
    gcol16 = kb.sb('gcol16', [16, 1], F32)
    t1b = kb.sb('t1b', [16, 128], BF16)
    t2b = kb.sb('t2b', [16, 128], BF16)
    rf = kb.sb('rf', [128, 128], F32)
    rr = kb.sb('rr', [128, 32 * 128], BF16)
    ysb2 = kb.sb('ysb2', [128, 128], F32)
    ftile = kb.sb('ftile', [128, 512], F32)
    rf2 = kb.sb('rf2', [128, 512], F32)
    qkvb = kb.sb('qkvb', [128, 512], BF16)
    sqb = kb.sb('sqb', [128, 512], F32)

    pmm = [kb.ps(f'pmm{i}', [128, 512], F32) for i in range(2)]
    pt1 = kb.ps('pt1', [16, 128], F32)
    pt2 = kb.ps('pt2', [16, 128], F32)
    pg = kb.ps('pg', [16, 1], F32)
    ptr_b = kb.ps('ptr_b', [128, 128], BF16)
    ptr_f = kb.ps('ptr_f', [128, 128], F32)
    pst = kb.ps('pst', [2, 512], F32)

    load_wpool(kb, pool, wo3T, D, D, stage, 'wo3', pool_off=65536)
    kb.dma('sp', bosb[:], bo3)
    kb.dma('sp', bsb4[:], bqkv4)
    for kc in range(8):
        kb.dma('sp', stage[:, 0:16], vT[kc * 128:(kc + 1) * 128, :])
        kb.act(vsb[:, kc * 16:(kc + 1) * 16], stage[:, 0:16], AF.Identity)
    kb.dma('sp', stage[0:16, 0:16], fbT)
    kb.act(fsb[:], stage[0:16, 0:16], AF.Identity)
    kb.dma('sp', gsb[:], gatev)
    kb.mm(pg[:], onesrowf[0:1, 0:16], gsb[:], start=True, stop=True)
    kb.dcopy(gcol16[:], pg[:])

    for r in range(32):
        kb.dma('sp', stage[:], w_fc[r * 128:(r + 1) * 128, :])
        kb.red(wsc[:], stage[:], ALU.add, absval=True)
        kb.ts(wsc[:], wsc[:], 1.0 / D, ALU.mult)
        kb.tss(wsc[:], wsc[:], 1e-5, ALU.max)
        kb.recip(wrs[:], wsc[:])
        kb.act(ztl[:], stage[:], AF.Identity, scale=wrs[:])
        kb.tss(ttl[:], ztl[:], 0.5, ALU.is_gt)
        kb.tss(ztl[:], ztl[:], -0.5, ALU.is_lt)
        kb.tt(ttl[:], ttl[:], ztl[:], ALU.subtract)
        kb.act(wb[:], ttl[:], AF.Identity, scale=wsc[:])
        for kc in range(8):
            kb.transp(ptr_b[:], wb[:, kc * 128:(kc + 1) * 128], identb[:])
            kb.dcopy(pool[:, kc * 4096 + r * 128: kc * 4096 + r * 128 + 128], ptr_b[:])
    for r in range(8):
        for cc in range(4):
            kb.dma('sp', stage[:], w_proj[r * 128:(r + 1) * 128, cc * 1024:(cc + 1) * 1024])
            kb.red(s4[:, cc:cc + 1], stage[:], ALU.add, absval=True)
        kb.red(wsc[:], s4[:], ALU.add)
        kb.ts(wsc[:], wsc[:], 1.0 / 4096.0, ALU.mult)
        kb.tss(wsc[:], wsc[:], 1e-5, ALU.max)
        kb.recip(wrs[:], wsc[:])
        for cc in range(4):
            kb.dma('sp', stage[:], w_proj[r * 128:(r + 1) * 128, cc * 1024:(cc + 1) * 1024])
            kb.act(ztl[:], stage[:], AF.Identity, scale=wrs[:])
            kb.tss(ttl[:], ztl[:], 0.5, ALU.is_gt)
            kb.tss(ztl[:], ztl[:], -0.5, ALU.is_lt)
            kb.tt(ttl[:], ttl[:], ztl[:], ALU.subtract)
            kb.act(wb[:], ttl[:], AF.Identity, scale=wsc[:])
            for j in range(8):
                kb.transp(ptr_b[:], wb[:, j * 128:(j + 1) * 128], identb[:])
                kc = cc * 8 + j
                kb.dcopy(pool[:, 32768 + kc * 1024 + r * 128: 32768 + kc * 1024 + r * 128 + 128],
                         ptr_b[:])

    for st in range(NST):
        for tt in range(4):
            ti = st * 4 + tt
            kb.dma('sp', xs[tt][:], x[ti * 128:(ti + 1) * 128, :])
        for kc in range(8):
            kb.dma('sp', otb[:, kc * 512:(kc + 1) * 512],
                   ot[kc * 128:(kc + 1) * 128, st * 512:(st + 1) * 512])
        for oc in range(8):
            p = pmm[oc % 2]
            for kc in range(8):
                kb.mm(p[:], pool[:, 65536 + kc * D + oc * 128: 65536 + kc * D + oc * 128 + 128],
                      otb[:, kc * 512:(kc + 1) * 512], start=(kc == 0), stop=(kc == 7))
            kb.act(ysb[:], p[:], AF.Identity, bias=bosb[:, oc:oc + 1])
            for tt in range(4):
                kb.transp(ptr_f[:], ysb[:, tt * 128:(tt + 1) * 128], identf[:])
                kb.tt(xs[tt][:, oc * 128:(oc + 1) * 128], xs[tt][:, oc * 128:(oc + 1) * 128],
                      ptr_f[:], ALU.add)
        _ln_xnt(kb, st, xs, xn, lnt, xnT, ptr_b, identb)
        for qr in range(4):
            def xsl(kc):
                return xnT[:, kc * 512 + qr * 128: kc * 512 + qr * 128 + 128]
            for kc in range(8):
                kb.mm(pt1[:], vsb[:, kc * 16:(kc + 1) * 16], xsl(kc),
                      start=(kc == 0), stop=(kc == 7))
            kb.dcopy(t1b[:], pt1[:])
            kb.mm(pt2[:], fsb[:], t1b[:], start=True, stop=True)
            kb.act(t2b[:], pt2[:], AF.Identity, scale=gcol16[:])
            for oc in range(32):
                p = pmm[oc % 2]
                kb.dma('sp', stage[0:16, 0:128], uT[:, oc * 128:(oc + 1) * 128])
                kb.act(ub1[:], stage[0:16, 0:128], AF.Identity)
                for kc in range(8):
                    kb.mm(p[:, 0:128], wblk(pool, 4096, kc, oc), xsl(kc),
                          start=(kc == 0), stop=False)
                kb.mm(p[:, 0:128], ub1[:], t2b[:], start=False, stop=True)
                kb.act(rf[:], p[:, 0:128], AF.Relu)
                kb.act(rr[:, oc * 128:(oc + 1) * 128], rf[:], AF.Square)
            for oc in range(8):
                p = pmm[oc % 2]
                for kc in range(32):
                    kb.mm(p[:, 0:128], pool[:, 32768 + kc * 1024 + oc * 128: 32768 + kc * 1024 + oc * 128 + 128],
                          rr[:, kc * 128:(kc + 1) * 128], start=(kc == 0), stop=(kc == 31))
                kb.act(ysb2[:], p[:, 0:128], AF.Identity)
                kb.transp(ptr_f[:], ysb2[:], identf[:])
                kb.tt(xs[qr][:, oc * 128:(oc + 1) * 128], xs[qr][:, oc * 128:(oc + 1) * 128],
                      ptr_f[:], ALU.add)
        for tt in range(4):
            ti = st * 4 + tt
            kb.dma('pool', xo[ti * 128:(ti + 1) * 128, :], xs[tt][:], outdma=True)
        for tt in range(4):
            ti = st * 4 + tt
            ln_tile(kb, xs[tt][:], xn[tt][:], lnt)
            kb.dma('pool', xn2o[ti * 128:(ti + 1) * 128, :], xn[tt][:], outdma=True)

    load_wpool(kb, pool, wqkv4, D, 3 * D, stage, 'wqkv4')
    for st in range(NST):
        kb.dma('sp', ftile[:], fT2[:, st * 512:(st + 1) * 512])
        for tt in range(4):
            ti = st * 4 + tt
            kb.dma('sp', xn[tt][:], xn2o[ti * 128:(ti + 1) * 128, :])
            for fc in range(8):
                kb.transp(ptr_b[:], xn[tt][:, fc * 128:(fc + 1) * 128], identb[:])
                kb.dcopy(xnT[:, fc * 512 + tt * 128: fc * 512 + tt * 128 + 128], ptr_b[:])
        for oc in range(24):
            p = pmm[oc % 2]
            for kc in range(8):
                kb.mm(p[:], wblk(pool, 3 * D, kc, oc), xnT[:, kc * 512:(kc + 1) * 512],
                      start=(kc == 0), stop=(kc == 7))
            dst = [q4t, k4t, v4t][oc // 8]
            if oc < 16:
                kb.act(rf2[:], p[:], AF.Identity, bias=bsb4[:, oc:oc + 1])
                kb.tt(rf2[:], rf2[:], ftile[:], ALU.mult)
                kb.dcopy(qkvb[:], rf2[:])
                kb.act(sqb[:], rf2[:], AF.Square)
                kb.mm(pst[:], ones2[:], sqb[:], start=True, stop=True)
                kb.dcopy(ysb[0:2, :], pst[:])
                dsts = sqq4 if oc < 8 else sqk4
                kb.dma('act', dsts[(oc % 8) * 2:(oc % 8) * 2 + 2, st * 512:(st + 1) * 512],
                       ysb[0:2, :], outdma=True)
            else:
                kb.act(qkvb[:], p[:], AF.Identity, bias=bsb4[:, oc:oc + 1])
            kb.dma('pool', dst[(oc % 8) * 128:(oc % 8) * 128 + 128, st * 512:(st + 1) * 512],
                   qkvb[:], outdma=True)
    return kb


def build_E4():
    kb = KB("E4")
    hc = host_consts()
    identb = kb.constsb('identb', hc['ident_b'])
    identf = kb.constsb('identf', hc['ident_f'])
    onesrowf = kb.constsb('onesrowf', hc['ones_row_f'])
    onescolf = kb.constsb('onescolf', np.ones((128, 1), dtype=np.float32))
    iota16 = kb.constsb('iota16', np.broadcast_to(np.arange(16, dtype=np.float32), (128, 16)).copy())
    iotad16 = kb.constsb('iotad16', np.broadcast_to(15.0 - np.arange(16, dtype=np.float32), (128, 16)).copy())
    sutb = kb.constsb('sutb', (np.arange(128)[:, None] < np.arange(128)[None, :]).astype(np.float32).astype(BF))

    x = kb.din('x', [TPC, D])
    ot = kb.din('ot', [D, TPC], BF16)
    wp4T = kb.din('wp4T', [D, D])
    bp4 = kb.din('bp4', [128, 8])
    wg = kb.din('wg', [D, 16])
    bg4 = kb.din('bg4', [1, 16])

    xo = kb.dout('xo', [TPC, D])
    xn7o = kb.dout('xn7o', [TPC, D], BF16)
    idxo = kb.dout('idxo', [128, 16], F32)
    poso = kb.dout('poso', [128, 16], F32)
    gateo = kb.dout('gateo', [128, 16], F32)
    cnto = kb.dout('cnto', [1, 16], F32)

    pool = kb.sb('pool', [128, 8192], BF16)
    stage = kb.sb('stage', [128, 1024], F32)
    xs = [kb.sb(f'x{i}', [128, D], F32) for i in range(4)]
    xn = [kb.sb(f'xn{i}', [128, D], BF16) for i in range(4)]
    lnt = LNTmp(kb, 'ln')
    xnT = kb.sb('xnT', [128, 8 * 512], BF16)
    otb = kb.sb('otb', [128, 8 * 512], BF16)
    ysb = kb.sb('ysb', [128, 512], F32)
    bosb = kb.sb('bosb', [128, 8], F32)
    gwsb = kb.sb('gwsb', [128, 8 * 16], BF16)
    gbb = kb.sb('gbb', [128, 16], F32)
    logit = kb.sb('logit', [128, 16], F32)
    mrow = kb.sb('mrow', [128, 1], F32)
    nmrow = kb.sb('nmrow', [128, 1], F32)
    zrow = kb.sb('zrow', [128, 1], F32)
    zre = kb.sb('zre', [128, 1], F32)
    prob = kb.sb('prob', [128, 16], F32)
    gsel = kb.sb('gsel', [128, 16], F32)
    psel = kb.sb('psel', [128, 16], F32)
    isel = kb.sb('isel', [128, 16], F32)
    eq1 = kb.sb('eq1', [128, 16], F32)
    tmp16 = kb.sb('tmp16', [128, 16], F32)
    md = kb.sb('md', [128, 1], F32)
    idxc = kb.sb('idxc', [128, 1], F32)
    ohsel = kb.sb('ohsel', [128, 16], BF16)
    ohself = kb.sb('ohself', [128, 16], F32)
    runrow = kb.sb('runrow', [1, 16], F32)
    r1tmp = kb.sb('r1tmp', [1, 16], F32)
    postile = kb.sb('postile', [128, 16], F32)
    p1 = kb.sb('p1', [128, 1], F32)

    pmm = [kb.ps(f'pmm{i}', [128, 512], F32) for i in range(2)]
    ptr_b = kb.ps('ptr_b', [128, 128], BF16)
    ptr_f = kb.ps('ptr_f', [128, 128], F32)
    ppos = kb.ps('ppos', [128, 16], F32)
    pbc = kb.ps('pbc', [128, 16], F32)
    pcnt = kb.ps('pcnt', [1, 16], F32)

    load_wpool(kb, pool, wp4T, D, D, stage, 'wp4')
    kb.dma('sp', bosb[:], bp4)
    for kc in range(8):
        kb.dma('sp', stage[:, 0:16], wg[kc * 128:(kc + 1) * 128, :])
        kb.act(gwsb[:, kc * 16:(kc + 1) * 16], stage[:, 0:16], AF.Identity)
    kb.dma('sp', stage[0:1, 0:16], bg4)
    kb.mm(pbc[:], onesrowf[:], stage[0:1, 0:16], start=True, stop=True)
    kb.dcopy(gbb[:], pbc[:])
    kb.memset(runrow[:], 0.0)

    for st in range(NST):
        for tt in range(4):
            ti = st * 4 + tt
            kb.dma('sp', xs[tt][:], x[ti * 128:(ti + 1) * 128, :])
        for kc in range(8):
            kb.dma('sp', otb[:, kc * 512:(kc + 1) * 512],
                   ot[kc * 128:(kc + 1) * 128, st * 512:(st + 1) * 512])
        _epilogue_wo(kb, st, otb, pool, D, bosb, xs, ysb, pmm, ptr_f, identf, x)
        for tt in range(4):
            ti = st * 4 + tt
            kb.dma('pool', xo[ti * 128:(ti + 1) * 128, :], xs[tt][:], outdma=True)
        _ln_xnt(kb, st, xs, xn, lnt, xnT, ptr_b, identb)
        for tt in range(4):
            ti = st * 4 + tt
            kb.dma('pool', xn7o[ti * 128:(ti + 1) * 128, :], xn[tt][:], outdma=True)
        for tt in range(4):
            ti = st * 4 + tt
            p = pmm[0]
            for kc in range(8):
                kb.mm(p[:, 0:16], xnT[:, kc * 512 + tt * 128: kc * 512 + tt * 128 + 128],
                      gwsb[:, kc * 16:(kc + 1) * 16], start=(kc == 0), stop=(kc == 7))
            kb.dcopy(logit[:], p[:, 0:16])
            kb.tt(logit[:], logit[:], gbb[:], ALU.add)
            kb.red(mrow[:], logit[:], ALU.max)
            kb.ts(nmrow[:], mrow[:], -1.0, ALU.mult)
            kb.act(prob[:], logit[:], AF.Exp, bias=nmrow[:], accum_out=zrow[:])
            kb.recip(zre[:], zrow[:])
            kb.act(prob[:], prob[:], AF.Identity, scale=zre[:])
            kb.red(mrow[:], prob[:], ALU.max)
            kb.ts(eq1[:], prob[:], mrow[:], ALU.is_equal)
            kb.tt(tmp16[:], eq1[:], iotad16[:], ALU.mult)
            kb.red(md[:], tmp16[:], ALU.max)
            kb.ts(idxc[:], md[:], -1.0, ALU.mult, 15.0, ALU.add)
            kb.ts(ohself[:], iota16[:], idxc[:], ALU.is_equal)
            kb.dcopy(ohsel[:], ohself[:])
            kb.mm(ppos[:], sutb[:], ohsel[:], start=True, stop=True)
            kb.mm(pbc[:], onesrowf[:], runrow[:], start=True, stop=True)
            kb.dcopy(postile[:], ppos[:])
            kb.tt(postile[:], postile[:], pbc[:], ALU.add)
            kb.tt(tmp16[:], postile[:], ohself[:], ALU.mult)
            kb.red(p1[:], tmp16[:], ALU.add)
            kb.dcopy(gsel[:, ti:ti + 1], mrow[:])
            kb.dcopy(isel[:, ti:ti + 1], idxc[:])
            kb.dcopy(psel[:, ti:ti + 1], p1[:])
            kb.mm(pcnt[:], onescolf[:], ohself[:], start=True, stop=True)
            kb.tt(r1tmp[:], runrow[:], pcnt[:], ALU.add)
            kb.dcopy(runrow[:], r1tmp[:])
    kb.dma('pool', idxo, isel[:], outdma=True)
    kb.dma('pool', poso, psel[:], outdma=True)
    kb.dma('pool', gateo, gsel[:], outdma=True)
    kb.dma('pool', cnto, runrow[:], outdma=True)
    return kb


def build_M():
    kb = KB("M")
    hc = host_consts()
    identb = kb.constsb('identb', hc['ident_b'])
    identf = kb.constsb('identf', hc['ident_f'])
    onesrowf = kb.constsb('onesrowf', hc['ones_row_f'])
    iota16 = kb.constsb('iota16', np.broadcast_to(np.arange(16, dtype=np.float32), (128, 16)).copy())
    iota1024 = kb.constsb('iota1024', np.broadcast_to(np.arange(1024, dtype=np.float32), (128, 1024)).copy())
    iota128c = kb.constsb('iota128c', np.arange(128, dtype=np.float32).reshape(128, 1))
    sut8 = kb.constsb('sut8', (np.arange(8)[:, None] < np.arange(8)[None, :]).astype(np.float32))

    xn7 = kb.din('xn7', [N, D], BF16)
    idxa_d = kb.din('idxa', [128, 128], F32)
    posa_d = kb.din('posa', [128, 128], F32)
    gatea_d = kb.din('gatea', [128, 128], F32)
    counts8 = kb.din('counts8', [8, 16], F32)
    ecol = kb.din('ecol', [128, 2], F32)
    w1p = kb.din('w1p', [2, D, 4096], F32)
    w2p = kb.din('w2p', [2, 4096, D], F32)
    b1p = kb.din('b1p', [2, 128, 32], F32)
    b2p = kb.din('b2p', [2, 128, 8], F32)

    y2 = kb.dout('y2', [2048, D], BF16)
    desto = kb.dout('desto', [128, 128], F32)
    gko = kb.dout('gko', [128, 128], F32)

    pool = kb.sb('pool', [128, 32768], BF16)
    hb = kb.sb('hb', [128, 32 * 1024], BF16)
    bufT = kb.sb('bufT', [128, 8 * 1024], BF16)
    yfm = kb.sb('yfm', [128, 8 * 1024], BF16)
    stage = kb.sb('stage', [128, 1024], F32)
    idxa = kb.sb('idxs', [128, 128], F32)
    posa = kb.sb('poss', [128, 128], F32)
    gatea = kb.sb('gates', [128, 128], F32)
    posg = kb.sb('posg', [128, 128], F32)
    keep = kb.sb('keep', [128, 128], F32)
    cnts = kb.sb('cnts', [8, 16], F32)
    offsT = kb.sb('offsT', [16, 8], F32)
    orow = kb.sb('orow', [1, 16], F32)
    obc = kb.sb('obc', [128, 16], F32)
    ohsel = kb.sb('ohsel', [128, 16], F32)
    t1 = kb.sb('t1', [128, 1], F32)
    t2 = kb.sb('t2', [128, 1], F32)
    t3 = kb.sb('t3', [128, 1], F32)
    peq = kb.sb('peq', [128, 1024], F32)
    wv = kb.sb('wv', [128, 1], F32)
    tokof = kb.sb('tokof', [128, 8], F32)
    toki = kb.sb('toki', [128, 8], I32)
    buf1 = kb.sb('buf1', [128, 1024], BF16)
    b1sb = kb.sb('b1sb', [128, 32], F32)
    b2sb = kb.sb('b2sb', [128, 8], F32)
    ysb = kb.sb('ysb', [128, 512], F32)
    ybf = kb.sb('ybf', [128, 512], BF16)
    destsb = kb.sb('destsb', [128, 128], F32)
    gksb = kb.sb('gksb', [128, 128], F32)
    esb = kb.sb('esb', [128, 2], F32)

    pmm = [kb.ps(f'pmm{i}', [128, 512], F32) for i in range(2)]
    ptok = kb.ps('ptok', [128, 8], F32)
    poffs = kb.ps('poffs', [16, 8], F32)
    pr1 = kb.ps('pr1', [1, 16], F32)
    pbc16 = kb.ps('pbc16', [128, 16], F32)
    ptr_b = kb.ps('ptr_b', [128, 128], BF16)

    kb.dma('sp', idxa[:], idxa_d)
    kb.dma('sp', posa[:], posa_d)
    kb.dma('sp', gatea[:], gatea_d)
    kb.dma('sp', cnts[:], counts8)
    kb.dma('sp', esb[:], ecol)
    kb.mm(poffs[:], cnts[:], sut8[0:8, 0:8], start=True, stop=True)
    kb.dcopy(offsT[:], poffs[:])

    for g in range(8):
        kb.transp(pr1[:], offsT[:, g:g + 1], identf[:])
        kb.dcopy(orow[:], pr1[:])
        kb.mm(pbc16[:], onesrowf[:], orow[:], start=True, stop=True)
        kb.dcopy(obc[:], pbc16[:])
        for j in range(g * 16, g * 16 + 16):
            kb.ts(ohsel[:], iota16[:], idxa[:, j:j + 1], ALU.is_equal)
            kb.tt(ohsel[:], ohsel[:], obc[:], ALU.mult)
            kb.red(t1[:], ohsel[:], ALU.add)
            kb.tt(posg[:, j:j + 1], posa[:, j:j + 1], t1[:], ALU.add)
            kb.tss(keep[:, j:j + 1], posg[:, j:j + 1], 1024.0, ALU.is_lt)
            kb.tss(t2[:], posg[:, j:j + 1], 1023.0, ALU.min)
            kb.stt(t3[:], idxa[:, j:j + 1], 1024.0, t2[:], ALU.mult, ALU.add)
            kb.tt(t2[:], t3[:], keep[:, j:j + 1], ALU.mult)
            kb.ts(t3[:], keep[:, j:j + 1], -1.0, ALU.mult, 1.0, ALU.add)
            kb.stt(destsb[:, j:j + 1], t3[:], 1.0e9, t2[:], ALU.mult, ALU.add)
            kb.tt(gksb[:, j:j + 1], keep[:, j:j + 1], gatea[:, j:j + 1], ALU.mult)
    kb.dma('pool', desto, destsb[:], outdma=True)
    kb.dma('pool', gko, gksb[:], outdma=True)

    for el in range(2):
        for j in range(128):
            kb.ts(t1[:], idxa[:, j:j + 1], esb[:, el:el + 1], ALU.is_equal)
            kb.tt(t1[:], t1[:], keep[:, j:j + 1], ALU.mult)
            kb.tss(t2[:], iota128c[:], 128.0 * j, ALU.add)
            kb.tt(wv[:], t2[:], t1[:], ALU.mult)
            kb.ts(peq[:], iota1024[:], posg[:, j:j + 1], ALU.is_equal)
            kb.ts(peq[:], peq[:], t1[:], ALU.mult)
            for pc in range(8):
                kb.mm(ptok[:, pc:pc + 1], peq[:, pc * 128:(pc + 1) * 128], wv[:],
                      start=(j == 0), stop=(j == 127))
        kb.dcopy(tokof[:], ptok[:])
        kb.dcopy(toki[:], tokof[:])
        for pc in range(8):
            kb.idma_gather(buf1[:], xn7, toki[:, pc:pc + 1], N - 1)
            for fc in range(8):
                kb.transp(ptr_b[:], buf1[:, fc * 128:(fc + 1) * 128], identb[:])
                kb.dcopy(bufT[:, fc * 1024 + pc * 128: fc * 1024 + pc * 128 + 128], ptr_b[:])
        load_wpool(kb, pool, w1p[el], D, 4096, stage, 'w1')
        kb.dma('sp', b1sb[:], b1p[el])
        kb.dma('sp', b2sb[:], b2p[el])
        for oc in range(32):
            for half in range(2):
                p = pmm[half]
                for kc in range(8):
                    kb.mm(p[:], wblk(pool, 4096, kc, oc),
                          bufT[:, kc * 1024 + half * 512: kc * 1024 + half * 512 + 512],
                          start=(kc == 0), stop=(kc == 7))
                kb.act(hb[:, oc * 1024 + half * 512: oc * 1024 + half * 512 + 512],
                       p[:], AF.Relu, bias=b1sb[:, oc:oc + 1])
        load_wpool(kb, pool, w2p[el], 4096, D, stage, 'w2')
        for oc in range(8):
            for half in range(2):
                p = pmm[half]
                for kc in range(32):
                    kb.mm(p[:], wblk(pool, D, kc, oc),
                          hb[:, kc * 1024 + half * 512: kc * 1024 + half * 512 + 512],
                          start=(kc == 0), stop=(kc == 31))
                kb.act(ybf[:], p[:], AF.Identity, bias=b2sb[:, oc:oc + 1])
                for piece in range(4):
                    pc = half * 4 + piece
                    kb.transp(ptr_b[:], ybf[:, piece * 128:(piece + 1) * 128], identb[:])
                    kb.dcopy(yfm[:, pc * 1024 + oc * 128: pc * 1024 + oc * 128 + 128], ptr_b[:])
        for pc in range(8):
            kb.dma('pool', y2[el * 1024 + pc * 128: el * 1024 + (pc + 1) * 128, :],
                   yfm[:, pc * 1024:(pc + 1) * 1024], outdma=True)
    return kb



def build_F():
    kb = KB("F")
    x = kb.din('x', [TPC, D])
    y2all = kb.din('y2all', [N, D], BF16)
    destf = kb.din('destf', [128, 16], F32)
    gkf = kb.din('gkf', [128, 16], F32)
    out = kb.dout('out', [TPC, D], F32)

    dsb = kb.sb('dsb', [128, 16], F32)
    gsb = kb.sb('gsb', [128, 16], F32)
    di = kb.sb('di', [128, 16], I32)
    ytile = kb.sb('ytile', [128, D], BF16)
    xtile = kb.sb('xtile', [128, D], F32)
    otile = kb.sb('otile', [128, D], F32)

    kb.dma('sp', dsb[:], destf)
    kb.dma('sp', gsb[:], gkf)
    kb.dcopy(di[:], dsb[:])
    for j in range(16):
        kb.memset(ytile[:], 0.0)
        kb.idma_gather(ytile[:], y2all, di[:, j:j + 1], N - 1)
        kb.dma('sp', xtile[:], x[j * 128:(j + 1) * 128, :])
        kb.stt(otile[:], ytile[:], gsb[:, j:j + 1], xtile[:], ALU.mult, ALU.add)
        kb.dma('pool', out[j * 128:(j + 1) * 128, :], otile[:], outdma=True)
    return kb


def _rope_ft2():
    d = 64
    theta = 1.0 / 10000.0 ** (2.0 * np.arange(0, d // 2, dtype=np.float32) / d)
    seq = np.arange(1, S + 1, dtype=np.float32)
    ang = np.einsum('n,d->nd', seq, theta)
    ang = np.concatenate([ang, ang], axis=-1)
    f = (np.cos(ang) + np.sin(ang)).astype(np.float32)
    fT = f.T
    return np.ascontiguousarray(np.vstack([fT, fT]))


def _bcol(b, nch):
    return np.ascontiguousarray(np.asarray(b, np.float32).reshape(nch, 128).T)


DEBUG_TRACE = False
TRACE = {}


def _snap(name, xc):
    if DEBUG_TRACE:
        TRACE[name] = np.concatenate(xc, axis=0).copy()


def kernel(**inputs):
    inp = {k: np.asarray(v) for k, v in inputs.items()}
    x = np.ascontiguousarray(inp['L0_attn_hidden_state'].reshape(N, D).astype(np.float32))
    xc = shard_tokens(x)

    WqT = inp['L0_attn_Wq'].reshape(1024, 1024).T
    WkT = inp['L0_attn_Wk'].reshape(1024, 1024).T
    WvT = inp['L0_attn_Wv'].reshape(1024, 1024).T
    wq = np.ascontiguousarray(np.concatenate([WqT, WkT, WvT], axis=1).astype(np.float32))
    bflat = np.concatenate([inp['L0_attn_bq'].reshape(-1), inp['L0_attn_bk'].reshape(-1),
                            inp['L0_attn_bv'].reshape(-1)]).astype(np.float32)
    res = build_P0().run([dict(x=xc[c], wq=wq, bq=_bcol(bflat, 24)) for c in range(NCORES)])
    qt = [r['qt'] for r in res]; kt = [r['kt'] for r in res]; vt = [r['vt'] for r in res]
    sqq = [r['sqq'] for r in res]; sqk = [r['sqk'] for r in res]

    res = build_attn('A0').run([dict(q_in=assemble_heads(qt, '')[c], k_in=assemble_heads(kt, '')[c],
                                     v_in=assemble_heads(vt, '')[c], sqq_in=assemble_norm(sqq, '')[c],
                                     sqk_in=assemble_norm(sqk, '')[c]) for c in range(NCORES)])
    ot_pc = disassemble_o([r['o_out'] for r in res])

    e0_in = dict(
        woT=np.ascontiguousarray(inp['L0_attn_Wo'].T.astype(np.float32)),
        bo=_bcol(inp['L0_attn_bo'], 8),
        gwT=np.ascontiguousarray(inp['L0_ffn_gate_w'].T.astype(np.float32)),
        gb=np.ascontiguousarray(inp['L0_ffn_gate_b'].reshape(1, 8).astype(np.float32)),
        w1T=np.ascontiguousarray(inp['L0_ffn_w1'].transpose(0, 2, 1).astype(np.float32)),
        w2T=np.ascontiguousarray(inp['L0_ffn_w2'].transpose(0, 2, 1).astype(np.float32)),
        b1h=np.ascontiguousarray(inp['L0_ffn_b1'].reshape(8, 32, 128).transpose(0, 2, 1).astype(np.float32)),
        b2h=np.ascontiguousarray(inp['L0_ffn_b2'].reshape(8, 8, 128).transpose(0, 2, 1).astype(np.float32)),
        wq1=np.ascontiguousarray(np.concatenate([inp['L1_attn_Wq'].T, inp['L1_attn_Wk'].T,
                                                 inp['L1_attn_Wv'].T], axis=1).astype(np.float32)),
        bq1=_bcol(np.concatenate([inp['L1_attn_bq'], inp['L1_attn_bk'], inp['L1_attn_bv']]), 24),
    )
    res = build_E0().run([dict(x=xc[c], ot=ot_pc[c], **e0_in) for c in range(NCORES)])
    xc = [r['xo'] for r in res]
    _snap('x_l0f', xc)
    q1t = [r['q1t'] for r in res]
    kvo = [r['kvo'] for r in res]
    kso = [r['kso'] for r in res]

    e1_in = dict(
        stdp=np.ascontiguousarray(inp['L1_attn_stdp'].reshape(1, 16).astype(np.float32)),
        wo1T=np.ascontiguousarray(inp['L1_attn_Wo'].T.astype(np.float32)),
        bo1=_bcol(inp['L1_attn_bo'], 8),
        wqkv2=np.ascontiguousarray(np.concatenate([inp['L1_ffn_Wq'], inp['L1_ffn_Wk'],
                                                   inp['L1_ffn_Wv']], axis=1).astype(np.float32)),
    )
    res = build_E1().run([dict(x=xc[c], q1t=q1t[c],
                               kvp=np.ascontiguousarray(np.stack([kvo[(c // 4) * 4 + i] for i in range(4)])),
                               ksp=np.ascontiguousarray(np.stack([kso[(c // 4) * 4 + i] for i in range(4)])),
                               **e1_in) for c in range(NCORES)])
    xc = [r['xo'] for r in res]
    _snap('x_l1a', xc)
    q2t = [r['q2t'] for r in res]; k2t = [r['k2t'] for r in res]; v2t = [r['v2t'] for r in res]
    sqq2 = [r['sqq2'] for r in res]; sqk2 = [r['sqk2'] for r in res]

    res = build_attn_q('A2').run([dict(q_in=assemble_heads(q2t, '')[c], k_in=assemble_heads(k2t, '')[c],
                                       v_in=assemble_heads(v2t, '')[c], sqq_in=assemble_norm(sqq2, '')[c],
                                       sqk_in=assemble_norm(sqk2, '')[c]) for c in range(NCORES)])
    ot_pc = disassemble_o([r['o_out'] for r in res])

    perm = []
    for sec in range(3):
        for h in range(HEADS):
            perm.extend(range(h * 192 + sec * 64, h * 192 + sec * 64 + 64))
    perm = np.array(perm)
    wqkv3 = np.ascontiguousarray(inp['L2_attn_Wqkv'][perm].T.astype(np.float32))
    bqkv3 = _bcol(inp['L2_attn_bqkv'][perm], 24)
    e2_in = dict(
        wo2=np.ascontiguousarray(inp['L1_ffn_Wo'].astype(np.float32)),
        wff=np.ascontiguousarray(inp['L1_ffn_Wff'].astype(np.float32)),
        wqkv3=wqkv3, bqkv3=bqkv3,
    )
    res = build_E2().run([dict(x=xc[c], ot=ot_pc[c], **e2_in) for c in range(NCORES)])
    xc = [r['xo'] for r in res]
    _snap('x_l1f', xc)
    q3t = [r['q3t'] for r in res]; k3t = [r['k3t'] for r in res]; v3t = [r['v3t'] for r in res]
    sqq3 = [r['sqq3'] for r in res]; sqk3 = [r['sqk3'] for r in res]

    res = build_attn('A3').run([dict(q_in=assemble_heads(q3t, '')[c], k_in=assemble_heads(k3t, '')[c],
                                     v_in=assemble_heads(v3t, '')[c], sqq_in=assemble_norm(sqq3, '')[c],
                                     sqk_in=assemble_norm(sqk3, '')[c]) for c in range(NCORES)])
    ot_pc = disassemble_o([r['o_out'] for r in res])

    ft2 = _rope_ft2()
    e3_in = dict(
        wo3T=np.ascontiguousarray(inp['L2_attn_Wo'].T.astype(np.float32)),
        bo3=_bcol(inp['L2_attn_bo'], 8),
        w_fc=np.ascontiguousarray(inp['L2_ffn_w_fc'].astype(np.float32)),
        w_proj=np.ascontiguousarray(inp['L2_ffn_w_proj'].astype(np.float32)),
        uT=np.ascontiguousarray(inp['L2_ffn_u'].T.astype(np.float32)),
        vT=np.ascontiguousarray(inp['L2_ffn_v'].T.astype(np.float32)),
        fbT=np.ascontiguousarray(inp['L2_ffn_fast_b'].T.astype(np.float32)),
        gatev=np.ascontiguousarray(np.asarray(inp['L2_ffn_gate'], np.float32).reshape(1, 1)),
        wqkv4=np.ascontiguousarray(inp['L3_attn_W_attn'].T.astype(np.float32)),
        bqkv4=_bcol(inp['L3_attn_b_attn'], 24),
    )
    res = build_E3().run([dict(x=xc[c], ot=ot_pc[c],
                               fT2=np.ascontiguousarray(ft2[:, (c % 4) * TPC:(c % 4) * TPC + TPC]),
                               **e3_in) for c in range(NCORES)])
    xc = [r['xo'] for r in res]
    _snap('x_l2f', xc)
    q4t = [r['q4t'] for r in res]; k4t = [r['k4t'] for r in res]; v4t = [r['v4t'] for r in res]
    sqq4 = [r['sqq4'] for r in res]; sqk4 = [r['sqk4'] for r in res]

    res = build_attn('A4', causal=True).run(
        [dict(q_in=assemble_heads(q4t, '')[c], k_in=assemble_heads(k4t, '')[c],
              v_in=assemble_heads(v4t, '')[c], sqq_in=assemble_norm(sqq4, '')[c],
              sqk_in=assemble_norm(sqk4, '')[c]) for c in range(NCORES)])
    ot_pc = disassemble_o([r['o_out'] for r in res])

    e4_in = dict(
        wp4T=np.ascontiguousarray(inp['L3_attn_W_proj'].T.astype(np.float32)),
        bp4=_bcol(inp['L3_attn_b_proj'], 8),
        wg=np.ascontiguousarray(inp['L3_ffn_Wg'].astype(np.float32)),
        bg4=np.ascontiguousarray(inp['L3_ffn_bg'].reshape(1, 16).astype(np.float32)),
    )
    res = build_E4().run([dict(x=xc[c], ot=ot_pc[c], **e4_in) for c in range(NCORES)])
    xc = [r['xo'] for r in res]
    _snap('x_l3a', xc)
    xn7 = np.ascontiguousarray(np.concatenate([r['xn7o'] for r in res], axis=0))
    idxa = np.ascontiguousarray(np.concatenate([r['idxo'] for r in res], axis=1))
    posa = np.ascontiguousarray(np.concatenate([r['poso'] for r in res], axis=1))
    gatea = np.ascontiguousarray(np.concatenate([r['gateo'] for r in res], axis=1))
    counts8 = np.ascontiguousarray(np.concatenate([r['cnto'] for r in res], axis=0))

    W1 = inp['L3_ffn_W1'].astype(np.float32)
    W2 = inp['L3_ffn_W2'].astype(np.float32)
    b1 = inp['L3_ffn_b1'].astype(np.float32)
    b2 = inp['L3_ffn_b2'].astype(np.float32)
    m_cores = []
    for c in range(NCORES):
        es = [2 * c, 2 * c + 1]
        ecol = np.ascontiguousarray(np.broadcast_to(np.array(es, np.float32), (128, 2)).copy())
        m_cores.append(dict(
            xn7=xn7, idxa=idxa, posa=posa, gatea=gatea, counts8=counts8, ecol=ecol,
            w1p=np.ascontiguousarray(W1[es]),
            w2p=np.ascontiguousarray(W2[es]),
            b1p=np.ascontiguousarray(b1[es].reshape(2, 32, 128).transpose(0, 2, 1)),
            b2p=np.ascontiguousarray(b2[es].reshape(2, 8, 128).transpose(0, 2, 1)),
        ))
    res = build_M().run(m_cores)
    y2all = np.ascontiguousarray(np.concatenate([r['y2'] for r in res], axis=0))
    desta = res[0]['desto']
    gka = res[0]['gko']

    res = build_F().run([dict(x=xc[c], y2all=y2all,
                              destf=np.ascontiguousarray(desta[:, c * 16:(c + 1) * 16]),
                              gkf=np.ascontiguousarray(gka[:, c * 16:(c + 1) * 16]))
                         for c in range(NCORES)])
    out = np.concatenate([r['out'] for r in res], axis=0)
    return np.ascontiguousarray(out.reshape(B, S, D).astype(np.float32))
